# version 73
# baseline (speedup 1.0000x reference)
import numpy as np
from contextlib import ExitStack

import concourse.bass as bass
import concourse.mybir as mybir
from concourse.bass_utils import run_bass_kernel_spmd

F32 = mybir.dt.float32
BF16 = mybir.dt.bfloat16
F16 = mybir.dt.float16
I32 = mybir.dt.int32
AF = mybir.ActivationFunctionType
ALU = mybir.AluOpType
AX = mybir.AxisListType

B, L, D, H = 8, 8192, 1024, 4096
OD = 64
NPARAMS = OD * (OD - 1) // 2
TILE = 256
NT = L // TILE
DC = D // 128
HC = H // 128
RMS_EPS = 1e-6
LN_EPS = 1e-5
MAGIC = 0x5F3759DF
ROW_OFF = np.concatenate([[0], np.cumsum(OD - 1 - np.arange(OD))]).astype(int)


class Region:
    __slots__ = ("w", "rs")
    def __init__(self):
        self.w = None
        self.rs = []


class Sched:
    ESEM = {"tensor": "pe", "vector": "dve", "scalar": "act", "gpsimd": "pool"}
    ENGS = ["sync", "tensor", "vector", "scalar", "gpsimd"]

    def __init__(self, serial_all=False):
        self.cnt = {}
        self.prog = {e: [] for e in self.ENGS}
        self.waited = {e: {} for e in self.ENGS}
        self.burst = set()
        self.serial = set()
        self.regions = {}
        self.serial_all = serial_all
        self.last = None

    def R(self, name):
        if name not in self.regions:
            self.regions[name] = Region()
        return self.regions[name]

    def _need(self, eng, waits, dep):
        sem, val = dep
        if sem in self.burst:
            val = self.cnt[sem]
        if sem == self.ESEM.get(eng) and sem == "pe":
            return
        if self.waited[eng].get(sem, 0) >= val:
            return
        waits.append((sem, val))
        self.waited[eng][sem] = val

    def emit(self, eng, thunk, reads=(), writes=(), dma_sem=None):
        if dma_sem is not None:
            sem, inc = dma_sem, 16
        else:
            sem, inc = self.ESEM[eng], 1
        self.cnt.setdefault(sem, 0)
        waits = []
        if self.serial_all and self.last is not None:
            leng, lsem, lcnt = self.last
            if leng != eng or lsem != sem:
                self._need(eng, waits, (lsem, lcnt))
        if dma_sem is not None and sem in self.serial and self.cnt[sem] > 0:
            self._need(eng, waits, (sem, self.cnt[sem]))
        for r in reads:
            reg = self.R(r)
            if reg.w:
                self._need(eng, waits, reg.w)
        for w in writes:
            reg = self.R(w)
            if reg.w:
                self._need(eng, waits, reg.w)
            for rd in reg.rs:
                self._need(eng, waits, rd)
        self.cnt[sem] += inc
        cnt = self.cnt[sem]
        for r in reads:
            self.R(r).rs.append((sem, cnt))
        for w in writes:
            reg = self.R(w)
            reg.w = (sem, cnt)
            reg.rs = []
        self.prog[eng].append((waits, thunk, sem, inc))
        self.last = (eng, sem, cnt)

    def wait_only(self, eng, deps):
        self.prog[eng].append(([tuple(d) for d in deps], None, None, 0))

    def replay(self, block, sems):
        for eng in self.ENGS:
            items = self.prog[eng]
            if not items:
                continue
            def body(h, items=items):
                for waits, thunk, sem, inc in items:
                    for (s, v) in waits:
                        h.wait_ge(sems[s], v)
                    if thunk is not None:
                        thunk(h).then_inc(sems[sem], inc)
            getattr(block, eng)(body)


def build_nc():
    nc = bass.Bass("TRN2", target_bir_lowering=False)

    def din(name, shape):
        return nc.dram_tensor(name, list(shape), F32, kind="ExternalInput").ap()

    x_h = din("x", (L, D))
    w1t_h = din("w1t", (D, H))
    w2t_h = din("w2t", (H, D))
    womt_h = din("womt", (D, NPARAMS))
    womb_h = din("womb", (1, NPARAMS))
    ooutt_h = din("ooutt", (OD, D))
    oob_h = din("oob", (1, D))
    vint_h = din("vint", (D, OD))
    vib_h = din("vib", (OD, 1))
    voutt_h = din("voutt", (OD, D))
    vob_h = din("vob", (1, D))
    cb_h = din("cb", (256, OD))
    cbt_h = din("cbt", (OD, 256))
    nw_h = din("nw", (1, D))
    b1_h = din("b1", (1, H))
    b2_h = din("b2", (1, D))
    mix_h = din("mixes", (3, 1))
    eye_h = din("eye", (128, 128))
    out_h = nc.dram_tensor("out", [L, D], F32, kind="ExternalOutput").ap()

    es = ExitStack()
    def sb(name, shape, dt=F32):
        return es.enter_context(nc.sbuf_tensor(name, shape, dt))

    W1e = sb("W1e", [128, DC * H], F16)
    W2e = sb("W2e", [128, HC * D], F16)
    xb = [sb(f"xb{i}", [128, 2 * D]) for i in range(2)]
    tb = sb("tb", [128, 2 * D])
    womb_sb = tb[0:1, 0:NPARAMS]
    x1bf = [sb(f"x1bf{i}", [128, 2 * D], F16) for i in range(2)]
    x1Tb = [sb("x1T0", [128, 2 * D], F16),
            sb("x1T1", [128, 2 * D], F16)]
    hbf = [sb(f"hbf{i}", [128, TILE], F16) for i in range(2)]
    nw_bc = sb("nw_bc", [128, D])
    brow = sb("brow", [1, 2 * D])
    scrP = sb("scrP", [128, 3072])
    scrPb = scrP.ap().bitcast(F16)
    b2_bc = scrPb[:, 4096:5120]
    Mp = sb("Mp", [65, D], F16)
    ovt = sb("ovt", [64, D], F16)
    vout_s = sb("vout_s", [64, D], F16)
    vint_s = sb("vint_s", [128, OD * DC])
    cb_s = sb("cb_s", [128, 2 * OD], F16)
    cbt_s = sb("cbt_s", [64, 256])
    cb2 = sb("cb2", [64, 256])
    eyef = sb("eyef", [128, 128])
    eyeb = sb("eyeb", [128, 128], F16)
    b1col = sb("b1col", [128, HC])
    xhT = sb("xhT", [65, TILE], F16)
    ones16 = sb("ones16", [1, 128], F16)
    bst2 = sb("bst2", [128, 24])
    agg2 = sb("agg2", [128, 4])
    xh_bf = [sb(f"xh_bf{i}", [128, OD], F16) for i in range(2)]
    ones_r = sb("ones_r", [1, 128])
    ones_c = sb("ones_c", [128, 1])
    ones64 = sb("ones64", [64, 1])
    mixs = sb("mixs", [3, 1])
    mixr = sb("mixr", [1, 3])
    omx = sb("omx", [64, 1])
    hm_col = sb("hm_col", [128, DC])
    vqi = sb("vqi", [64, 1])
    code_c = sb("code_c", [64, 1], F16)
    mask_cf = sb("mask_cf", [128, 2], F16)
    sc_n = sb("sc_n", [1, 256])
    sc_mx = sb("sc_mx", [1, 1])
    mask_r = scrP[0:1, 1792:2048]
    mask_rb = sb("mask_rb", [1, 256], F16)
    _sp64 = x1bf[1].ap().bitcast(F32)
    Ub = _sp64[0:64, 0:64]
    Om = _sp64[0:64, 64:128]
    OmN = _sp64[0:64, 128:192]
    Om2 = _sp64[0:64, 192:256]
    Gt = _sp64[0:64, 256:320]
    Gtb = x1bf[1][0:64, 640:704]
    bst = sb("bst", [128, 24])
    agg = sb("agg", [128, 4])
    _st = sb("stat8", [128, 16])
    q_r = _st[:, 0:2]; rstd_r = _st[:, 2:4]
    q1 = _st[:, 4:6]; inv1 = _st[:, 6:8]; negm1 = _st[:, 8:10]
    q2 = _st[:, 10:12]; inv2 = _st[:, 12:14]; negm2 = _st[:, 14:16]
    nwi = sb("nwi", [128, 6], I32)
    nws = sb("nws", [128, 4])

    F01 = es.enter_context(nc.psum_tensor("F01", [128, 1024], F32))
    F23 = es.enter_context(nc.psum_tensor("F23", [128, 1024], F32))
    FP = [F01, F23]
    P = [None, None, None, None] + [
        es.enter_context(nc.psum_tensor(f"P{i}", [128, 512], F32)) for i in range(4, 8)]
    PB = [None, None, None, None] + [p.ap().bitcast(F16) for p in P[4:]]

    import os
    S = Sched(serial_all=bool(int(os.environ.get("KSERIAL", "0"))))
    S.serial.update({"dout"})
    S.burst.update({"xinA", "xinB", "xinC", "xinS", "cwA", "cwB",
                    "wload", "cload",
                    "scat", "rows", "rowsb", "rowsg"})
    E = S.emit

    def dma(eng, sem, dst, src, reads=(), writes=(), **kw):
        E(eng, lambda h, d=dst, s=src: h.dma_start(d, s, **kw),
          reads=reads, writes=writes, dma_sem=sem)

    hm_row = scrP[0:1, 0:D]
    rsn = scrP[0:1, 1280:1536]
    ni = scrP.ap().bitcast(I32)[0:1, 1536:1792]
    def womt_c(h):
        if h == 0:
            return scrP[:, 2048 + 0:2048 + 1008]
        if h == 1:
            return x1Tb[1].ap().bitcast(F32)[:, 0:1008]
        return x1bf[0].ap().bitcast(F32)[:, 0:1008]
    PAIR_LVL = {0: (0, 0), 1: (0, 1), 2: (0, 2), 3: (1, 0), 4: (1, 1),
                5: (1, 2), 6: (2, 0), 7: (2, 1)}
    def rowap(l):
        if l % 2 == 1:
            pi, which = (l - 1) // 2, 0
        else:
            pi, which = (l - 2) // 2, 1
        lvl, si = PAIR_LVL[pi]
        slot = si * 2 + which
        return scrPb[32 * lvl:32 * lvl + 1, slot * 1024:(slot + 1) * 1024]

    E("vector", lambda v: v.memset(ones_c[:], 1.0), writes=["ones_c"])
    NPA = L // 256
    paslot = [(xb[0][:], "xb0"), (xb[1][:], "xb1")]
    for ti in range(NPA):
        sl = ti % 2
        xbuf, xbr = paslot[sl]
        src = x_h.rearrange("(t p s) d -> t p (s d)", p=128, s=2)[ti]
        dma(["sync", "scalar"][sl], ["xinA", "xinS"][sl],
            xbuf, src, writes=[xbr])
        pr = (ti // 2) % 2
        acc = scrP[:, pr * D:(pr + 1) * D]
        if ti % 2 == 0:
            E("vector", lambda v, xbuf=xbuf, xbr=xbr, acc=acc: v.tensor_tensor(
                acc, xbuf[:, 0:D], xbuf[:, D:2 * D], op=ALU.add),
              reads=[xbr], writes=[f"paS{pr}"])
        else:
            for hh in range(2):
                E("vector", lambda v, xbuf=xbuf, xbr=xbr, acc=acc, hh=hh:
                  v.tensor_tensor(acc, acc, xbuf[:, hh * D:(hh + 1) * D],
                                  op=ALU.add),
                  reads=[xbr, f"paS{pr}"], writes=[f"paS{pr}"])
            def pa(t, acc=acc, first=(ti == 1), last=(ti == NPA - 1)):
                lm = None
                for nh in range(2):
                    lm = t.matmul(F01[0:1, nh * 512:(nh + 1) * 512], ones_c[:],
                                  acc[:, nh * 512:(nh + 1) * 512],
                                  start=first, stop=last,
                                  skip_group_check=True)
                return lm
            E("tensor", pa, reads=[f"paS{pr}", "ones_c"], writes=["F0a", "F0b"])

    dma("sync", "cload", eyef[:], eye_h, writes=["eyef"])
    dma("sync", "cload", nw_bc[0:1, :], nw_h, writes=["nw_pre"])
    dma("sync", "cload", brow[0:1, D:2 * D], oob_h, writes=["bias_row"])
    dma("sync", "cload", brow[0:1, 0:D], vob_h, writes=["vob_r"])
    dma("sync", "cload", womb_sb, womb_h, writes=["womb_sb", "tbA"])
    dma("sync", "cload", mixs[:], mix_h, writes=["mixs"])
    dma("sync", "cload", mixr[:], mix_h.rearrange("a b -> b a"), writes=["mixr"])
    dma("sync", "cload", vqi[:], vib_h, writes=["vqi"])
    dma("sync", "cload", cbt_s[:], cbt_h, writes=["cbt_s"])
    dma("sync", "cload", vint_s[:].rearrange("p (c v) -> p c v", v=OD),
        vint_h.rearrange("(c p) v -> p c v", p=128), writes=["vint_s"])
    dma("gpsimd", "wload", b1col[:], b1_h.rearrange("a (k p) -> a p k", p=128)[0],
        writes=["b1col"], allow_slow_non_contiguous=True)
    dma("gpsimd", "wload", ovt[:], ooutt_h, writes=["ovt"])
    dma("gpsimd", "wload", vout_s[:], voutt_h, writes=["vout_s"])
    dma("gpsimd", "wload", cb_s[:].rearrange("p (c v) -> p c v", v=OD),
        cb_h.rearrange("(c p) v -> p c v", p=128), writes=["cb_s"])
    for dc in range(DC):
        src = w1t_h.rearrange("(c p) h -> p c h", p=128)[:, dc:dc + 1, :]
        dma("gpsimd", f"wb{dc}",
            W1e[:, dc * H:(dc + 1) * H].rearrange("p (c h) -> p c h", h=H),
            src, writes=[f"W1d{dc}"])
    for j in range(8):
        src = w2t_h.rearrange("(c p) o -> p c o", p=128)[:, j * 4:(j + 1) * 4, :]
        dma("gpsimd", f"wc{j}",
            W2e[:, j * 4 * D:(j + 1) * 4 * D].rearrange("p (c o) -> p c o", o=D),
            src, writes=[f"W2d{j}"])

    E("vector", lambda v: v.memset(ones_r[:], 1.0), writes=["ones_r"])
    E("vector", lambda v: v.memset(ones16[:], 1.0), writes=["ones16"])
    E("vector", lambda v: v.memset(xhT[64:65, :], 1.0), writes=["xhT"])
    E("vector", lambda v: v.memset(ones64[:], 1.0), writes=["ones64"])
    E("vector", lambda v: v.tensor_copy(eyeb[:], eyef[:]), reads=["eyef"], writes=["eyeb"])
    for nh in range(2):
        E("tensor", lambda t, nh=nh: t.matmul(
            P[6 + nh][:, 0:512], ones_r[:], nw_bc[0:1, nh * 512:(nh + 1) * 512],
            start=True, stop=True, skip_group_check=True),
          reads=["ones_r", "nw_pre"], writes=[f"P{6 + nh}"])
        E("vector", lambda v, nh=nh: v.tensor_copy(
            nw_bc[:, nh * 512:(nh + 1) * 512], P[6 + nh][:, 0:512]),
          reads=[f"P{6 + nh}"], writes=["nw_bc", "nw_pre"])

    E("vector", lambda v: v.tensor_scalar(hm_row, F01[0:1, 0:1024], 1.0 / L, None,
                                          op0=ALU.mult),
      reads=["F0a", "F0b"], writes=["hmr", "paS0"])
    def hm_tr(t):
        lm = None
        for dc in range(DC):
            lm = t.transpose(F23[:, dc:dc + 1],
                             hm_row[0:1, dc * 128:(dc + 1) * 128], eyef[0:1, 0:1])
        return lm
    E("tensor", hm_tr, reads=["hmr", "eyef"], writes=["F1a"])
    E("vector", lambda v: v.tensor_copy(hm_col[:], F23[:, 0:DC]),
      reads=["F1a"], writes=["hm_col"])

    def pad_ap(r, c0, ln):
        g = r * 64 + c0
        if r < 16:
            return scrP[0:1, g:g + ln]
        if r < 48:
            return scrP[32:33, g - 1024:g - 1024 + ln]
        return scrP[64:65, g - 3072:g - 3072 + ln]
    E("vector", lambda v: v.memset(scrP[0:1, 0:1024], 0.0),
      writes=["ppad", "paS0"])
    E("vector", lambda v: v.memset(scrP[32:33, 0:2048], 0.0),
      writes=["ppad", "paS0", "paS1"])
    E("vector", lambda v: v.memset(scrP[64:65, 0:1024], 0.0),
      writes=["ppad", "paS0"])
    for jh in range(2):
        for dc in range(DC):
            ci = jh * DC + dc
            hbuf = ci % 3
            src = womt_h.rearrange("(h p) j -> h p j", p=128)[dc][
                :, jh * 1008:(jh + 1) * 1008]
            dma(["sync", "scalar"][ci % 2], ["cwA", "cwB"][ci % 2],
                womt_c(hbuf), src,
                reads=[f"scrW{hbuf}"], writes=[f"scrW{hbuf}"])
            def pm(t, jh=jh, dc=dc, hbuf=hbuf):
                lm = None
                for n in range(2):
                    lm = t.matmul(P[4 + jh * 2 + n][0:1, 0:504],
                                  hm_col[:, dc:dc + 1],
                                  womt_c(hbuf)[:, n * 504:(n + 1) * 504],
                                  start=(dc == 0), stop=False,
                                  skip_group_check=True)
                return lm
            E("tensor", pm, reads=["hm_col", f"scrW{hbuf}"],
              writes=[f"P{4 + jh * 2}", f"P{5 + jh * 2}"])
    def pm_bias(t):
        lm = None
        for jq in range(4):
            lm = t.matmul(P[4 + jq][0:1, 0:504], ones_c[0:1, 0:1],
                          womb_sb[0:1, jq * 504:(jq + 1) * 504],
                          start=False, stop=True, skip_group_check=True)
        return lm
    E("tensor", pm_bias, reads=["ones_c", "womb_sb"],
      writes=["P4", "P5", "P6", "P7"])
    for jq in range(4):
        ja_, jb_ = 504 * jq, 504 * (jq + 1)
        for r in range(OD - 1):
            a = max(ja_, int(ROW_OFF[r])); b = min(jb_, int(ROW_OFF[r + 1]))
            if a >= b:
                continue
            c0 = r + 1 + (a - int(ROW_OFF[r]))
            E("vector", lambda v, r=r, a=a, b=b, c0=c0, jq=jq: v.tensor_copy(
                pad_ap(r, c0, b - a), P[4 + jq][0:1, a - 504 * jq:b - 504 * jq]),
              reads=[f"P{4 + jq}"], writes=["ppad"])

    def u_tr(t):
        lm = None
        for r in range(OD):
            base = 0 if r < 16 else (32 if r < 48 else 64)
            lm = t.transpose(P[5][0:64, r:r + 1], pad_ap(r, 0, 64),
                             ones_c[base:base + 1, 0:1])
        return lm
    E("tensor", u_tr, reads=["ppad", "ppadB", "ppadC", "ones_c"],
      writes=["P5"])
    E("vector", lambda v: v.tensor_copy(Ub[:], P[5][0:64, 0:64]),
      reads=["P5"], writes=["Ub"])
    E("tensor", lambda t: t.transpose(P[5][0:64, 0:64], Ub[:], eyef[0:64, 0:64]),
      reads=["Ub", "eyef"], writes=["P5"])
    E("vector", lambda v: v.tensor_tensor(Om[:], P[5][0:64, 0:64], Ub[:], op=ALU.subtract),
      reads=["Ub", "P5"], writes=["Om"])
    E("vector", lambda v: v.tensor_tensor(OmN[:], Ub[:], P[5][0:64, 0:64], op=ALU.subtract),
      reads=["Ub", "P5"], writes=["OmN"])
    E("tensor", lambda t: t.matmul(P[5][0:64, 0:64], OmN[:], Om[:], start=True,
                                   stop=True, skip_group_check=True),
      reads=["OmN", "Om"], writes=["P5"])
    E("vector", lambda v: v.tensor_copy(Om2[:], P[5][0:64, 0:64]),
      reads=["P5"], writes=["Om2"])
    E("tensor", lambda t: t.matmul(P[5][0:64, 0:64], Om2[:], Om[:], start=True,
                                   stop=True, skip_group_check=True),
      reads=["Om2", "Om"], writes=["P5"])
    E("vector", lambda v: v.tensor_tensor(Gt[:], eyef[0:64, 0:64], Om[:], op=ALU.subtract),
      reads=["eyef", "Om"], writes=["Gt"])
    E("vector", lambda v: v.scalar_tensor_tensor(Gt[:], Om2[:], 0.5, Gt[:],
                                                 op0=ALU.mult, op1=ALU.add),
      reads=["Om2", "Gt"], writes=["Gt"])
    E("vector", lambda v: v.scalar_tensor_tensor(Gt[:], P[5][0:64, 0:64], -1.0 / 6.0,
                                                 Gt[:], op0=ALU.mult, op1=ALU.add),
      reads=["P5", "Gt"], writes=["Gt"])
    E("vector", lambda v: v.tensor_copy(Gtb[:], Gt[:]), reads=["Gt"], writes=["Gtb"])
    E("tensor", lambda t: t.matmul(P[4][0:64, 0:512], Gtb[:], ovt[:, 0:512],
                                   start=True, stop=True, skip_group_check=True),
      reads=["Gtb", "ovt"], writes=["P4"])
    E("tensor", lambda t: t.matmul(P[5][0:64, 0:512], Gtb[:], ovt[:, 512:1024],
                                   start=True, stop=True, skip_group_check=True),
      reads=["Gtb", "ovt"], writes=["P5"])
    E("tensor", lambda t: t.matmul(P[6][0:64, 0:1], ones_r[0:1, 0:64], mixs[0:1, 0:1],
                                   start=True, stop=True, skip_group_check=True),
      reads=["ones_r", "mixs"], writes=["P6"])
    E("vector", lambda v: v.tensor_copy(omx[:], P[6][0:64, 0:1]),
      reads=["P6"], writes=["omx"])
    E("vector", lambda v: v.tensor_scalar(Mp[0:64, 0:512], P[4][0:64, 0:512],
                                          omx[:, 0:1], None, op0=ALU.mult),
      reads=["P4", "omx"], writes=["Mp"])
    E("vector", lambda v: v.tensor_scalar(Mp[0:64, 512:1024], P[5][0:64, 0:512],
                                          omx[:, 0:1], None, op0=ALU.mult),
      reads=["P5", "omx"], writes=["Mp"])

    def vq_mm(t):
        lm = None
        for dc in range(DC):
            lm = t.matmul(P[6][0:64, 0:1], vint_s[:, dc * OD:(dc + 1) * OD],
                          hm_col[:, dc:dc + 1], start=(dc == 0), stop=(dc == DC - 1),
                          skip_group_check=True)
        return lm
    E("tensor", vq_mm, reads=["vint_s", "hm_col"], writes=["P6"])
    E("vector", lambda v: v.tensor_tensor(vqi[:], vqi[:], P[6][0:64, 0:1], op=ALU.add),
      reads=["P6", "vqi"], writes=["vqi"])
    E("vector", lambda v: v.tensor_tensor(cb2[:], cbt_s[:], cbt_s[:], op=ALU.mult),
      reads=["cbt_s"], writes=["cb2"])
    E("tensor", lambda t: t.matmul(P[7][0:1, 0:256], ones64[:], cb2[:], start=True,
                                   stop=True, skip_group_check=True),
      reads=["ones64", "cb2"], writes=["P7"])
    def emit_newton(qrg, q_ap, out_ap, outrg, int_ap, s1_ap, s2_ap, scr_rg):
        E("vector", lambda v: v.tensor_scalar(int_ap, q_ap.bitcast(I32), 1, None,
                                              op0=ALU.arith_shift_right),
          reads=[qrg], writes=[scr_rg])
        E("vector", lambda v: v.tensor_scalar(int_ap, int_ap, -1, MAGIC,
                                              op0=ALU.mult, op1=ALU.add),
          reads=[scr_rg], writes=[scr_rg])
        r = int_ap.bitcast(F32)
        for it in range(2):
            E("vector", lambda v: v.tensor_tensor(s1_ap, r, r, op=ALU.mult),
              reads=[scr_rg], writes=[scr_rg])
            E("vector", lambda v: v.tensor_tensor(s1_ap, s1_ap, q_ap, op=ALU.mult),
              reads=[scr_rg, qrg], writes=[scr_rg])
            E("vector", lambda v: v.tensor_scalar(s2_ap, s1_ap, -0.5, 1.5,
                                                  op0=ALU.mult, op1=ALU.add),
              reads=[scr_rg], writes=[scr_rg])
            dst, drg = (out_ap, outrg) if it == 1 else (r, scr_rg)
            E("vector", lambda v, dst=dst: v.tensor_tensor(dst, r, s2_ap, op=ALU.mult),
              reads=[scr_rg], writes=[drg])
    E("vector", lambda v: v.tensor_copy(rsn, P[7][0:1, 0:256]),
      reads=["P7"], writes=["rsn"])
    emit_newton("rsn", rsn, rsn, "rsn", ni, sc_n[0:1, 0:256],
                mask_r[0:1, 0:256], "nscr")
    E("tensor", lambda t: t.matmul(P[7][0:1, 0:256], vqi[:], cbt_s[:], start=True,
                                   stop=True, skip_group_check=True),
      reads=["vqi", "cbt_s"], writes=["P7"])
    E("vector", lambda v: v.tensor_tensor(sc_n[:], P[7][0:1, 0:256], rsn, op=ALU.mult),
      reads=["P7", "rsn"], writes=["sc_n"])
    E("vector", lambda v: v.reduce_max(sc_mx[:], sc_n[:], axis=AX.X),
      reads=["sc_n"], writes=["sc_mx"])
    E("vector", lambda v: v.tensor_scalar(mask_r[:], sc_n[:], sc_mx[0:1, 0:1], None,
                                          op0=ALU.is_ge),
      reads=["sc_n", "sc_mx"], writes=["mask_r"])
    def mask_tr(t):
        t.transpose(P[6][:, 0:1], mask_r[0:1, 0:128], eyef[0:1, 0:1])
        return t.transpose(P[6][:, 1:2], mask_r[0:1, 128:256], eyef[0:1, 0:1])
    E("tensor", mask_tr, reads=["mask_r", "eyef"], writes=["P6"])
    E("vector", lambda v: v.tensor_copy(mask_cf[:], P[6][:, 0:2]),
      reads=["P6"], writes=["mask_cf"])
    def code_mm(t):
        t.matmul(P[6][0:64, 0:1], cb_s[:, 0:64], mask_cf[:, 0:1], start=True,
                 stop=False, skip_group_check=True)
        return t.matmul(P[6][0:64, 0:1], cb_s[:, 64:128], mask_cf[:, 1:2], start=False,
                        stop=True, skip_group_check=True)
    E("tensor", code_mm, reads=["cb_s", "mask_cf"], writes=["P6"])
    E("vector", lambda v: v.tensor_copy(code_c[:], P[6][0:64, 0:1]),
      reads=["P6"], writes=["code_c"])
    def vqc_mm(t):
        t.matmul(P[6][0:1, 0:512], code_c[:], vout_s[:, 0:512], start=True,
                 stop=True, skip_group_check=True)
        return t.matmul(P[7][0:1, 0:512], code_c[:], vout_s[:, 512:1024], start=True,
                        stop=True, skip_group_check=True)
    E("tensor", vqc_mm, reads=["code_c", "vout_s"], writes=["P6", "P7"])
    bias_row = brow[0:1, D:2 * D]
    E("vector", lambda v: v.tensor_scalar(bias_row, bias_row, mixr[0:1, 0:1], None,
                                          op0=ALU.mult),
      reads=["bias_row", "mixr"], writes=["bias_row"])
    E("vector", lambda v: v.scalar_tensor_tensor(bias_row, brow[0:1, 0:D],
                                                 mixr[0:1, 1:2], bias_row,
                                                 op0=ALU.mult, op1=ALU.add),
      reads=["vob_r", "mixr", "bias_row"], writes=["bias_row"])
    for nh, pk in ((0, 6), (1, 7)):
        E("vector", lambda v, nh=nh, pk=pk: v.scalar_tensor_tensor(
            brow[0:1, D + nh * 512:D + (nh + 1) * 512], P[pk][0:1, 0:512],
            mixr[0:1, 1:2], brow[0:1, D + nh * 512:D + (nh + 1) * 512],
            op0=ALU.mult, op1=ALU.add),
          reads=[f"P{pk}", "mixr", "bias_row"], writes=["bias_row"])
    dma("sync", "cload2", brow[0:1, 0:D], b2_h, writes=["vob_r", "b2row"])
    E("vector", lambda v: v.tensor_copy(Mp[64:65, :], bias_row),
      reads=["bias_row"], writes=["Mp"])

    src0 = x_h.rearrange("(t p s) d -> t p (s d)", p=128, s=2)[0]
    dma("sync", "xinA", xb[0][:], src0, writes=["xb0"])


    def attn_steps(ti):
        sl = ti % 3
        xcur = xbl[sl]
        xreg = f"xb{sl}"
        xbf = x1bf[ti % 2]
        xbreg = f"x1bf{ti % 2}"
        steps = []
        def s_xh():
            for c in range(2):
                E("vector", lambda v, c=c, xcur=xcur: v.tensor_copy(
                    xh_bf[c][:], xcur[:, c * D:c * D + OD]),
                  reads=[xreg], writes=[f"xh_bf{c}"])
            for c in range(2):
                E("tensor", lambda t, c=c: t.transpose(
                    PB[6][0:64, 256 + c * 128:256 + (c + 1) * 128], xh_bf[c][:], eyeb[:]),
                  reads=[f"xh_bf{c}", "eyeb"], writes=["P6"])
            for c in range(2):
                E("scalar", lambda s, c=c: s.copy(
                    xhT[0:64, c * 128:(c + 1) * 128],
                    PB[6][0:64, 256 + c * 128:256 + (c + 1) * 128]),
                  reads=["P6"], writes=["xhT"])

        steps.append(s_xh)
        for c in range(2):
            for nh in range(2):
                def s_lie(c=c, nh=nh):
                    pk = 6 + nh
                    E("tensor", lambda t, c=c, nh=nh, pk=pk: t.matmul(
                        P[pk][:, 0:512], xhT[:, c * 128:(c + 1) * 128],
                        Mp[:, nh * 512:(nh + 1) * 512], start=True, stop=True,
                        skip_group_check=True),
                      reads=["xhT", "Mp"], writes=[f"P{pk}"])
                    E("vector", lambda v, c=c, nh=nh, pk=pk, xcur=xcur: v.tensor_tensor(
                        tb[:, c * D + nh * 512:c * D + (nh + 1) * 512], P[pk][:, 0:512],
                        xcur[:, c * D + nh * 512:c * D + (nh + 1) * 512], op=ALU.add),
                      reads=[f"P{pk}", xreg],
                      writes=["tb"] + (["tbA", "womb_sb"] if ti == 0 else []))
                steps.append(s_lie)
        def s_rms():
            for c in range(2):
                E("vector", lambda v, c=c: v.bn_stats(bst[:, c * 12:c * 12 + 6],
                                                      tb[:, c * D:c * D + 512]),
                  reads=["tb"], writes=["bst"])
                E("vector", lambda v, c=c: v.bn_stats(bst[:, c * 12 + 6:c * 12 + 12],
                                                      tb[:, c * D + 512:(c + 1) * D]),
                  reads=["tb"], writes=["bst"])
                E("vector", lambda v, c=c: v.bn_aggr(agg[:, c * 2:c * 2 + 2],
                                                     bst[:, c * 12:c * 12 + 12]),
                  reads=["bst"], writes=["agg"])
                E("vector", lambda v, c=c: v.tensor_tensor(
                    q_r[:, c:c + 1], agg[:, c * 2:c * 2 + 1], agg[:, c * 2:c * 2 + 1],
                    op=ALU.mult),
                  reads=["agg"], writes=["q_r"])
                E("vector", lambda v, c=c: v.tensor_tensor(
                    q_r[:, c:c + 1], q_r[:, c:c + 1], agg[:, c * 2 + 1:c * 2 + 2],
                    op=ALU.add),
                  reads=["agg", "q_r"], writes=["q_r"])
            E("vector", lambda v: v.tensor_scalar(q_r[:], q_r[:], RMS_EPS, None, op0=ALU.add),
              reads=["q_r"], writes=["q_r"])
            emit_newton("q_r", q_r[:], rstd_r[:], "rstd_r", nwi[:, 0:2],
                        nws[:, 0:2], nws[:, 2:4], "nwsc")
            for c in range(2):
                for nh in range(2):
                    s0 = c * D + nh * 512
                    E("vector", lambda v, c=c, s0=s0, nh=nh: v.scalar_tensor_tensor(
                        tb[:, s0:s0 + 512], tb[:, s0:s0 + 512], rstd_r[:, c:c + 1],
                        nw_bc[:, nh * 512:(nh + 1) * 512], op0=ALU.mult, op1=ALU.mult),
                      reads=["rstd_r", "nw_bc", "tb"], writes=["tb"])
            E("vector", lambda v, xcur=xcur: v.tensor_tensor(tb[:], tb[:], xcur,
                                                             op=ALU.add),
              reads=[xreg, "tb"], writes=["tb"])

        steps.append(s_rms)
        def s_ln1():
            for c in range(2):
                E("vector", lambda v, c=c: v.bn_stats(bst[:, c * 12:c * 12 + 6],
                                                      tb[:, c * D:c * D + 512]),
                  reads=["tb"], writes=["bst"])
                E("vector", lambda v, c=c: v.bn_stats(bst[:, c * 12 + 6:c * 12 + 12],
                                                      tb[:, c * D + 512:(c + 1) * D]),
                  reads=["tb"], writes=["bst"])
                E("vector", lambda v, c=c: v.bn_aggr(agg[:, c * 2:c * 2 + 2],
                                                     bst[:, c * 12:c * 12 + 12]),
                  reads=["bst"], writes=["agg"])
            agg_m = agg[:, 0:4].rearrange("p (c s) -> p c s", s=2)[:, :, 0]
            agg_v = agg[:, 0:4].rearrange("p (c s) -> p c s", s=2)[:, :, 1]
            E("vector", lambda v: v.tensor_scalar(q1[:], agg_v, LN_EPS, None, op0=ALU.add),
              reads=["agg"], writes=["q1"])
            E("vector", lambda v: v.tensor_scalar(negm1[:], agg_m, -1.0, None, op0=ALU.mult),
              reads=["agg"], writes=["negm1"])
            emit_newton("q1", q1[:], inv1[:], "inv1", nwi[:, 2:4],
                        nws[:, 0:2], nws[:, 2:4], "nwsc")

        steps.append(s_ln1)
        def s_ln1b():
            xbw = [xbreg] + (["scrW2"] if ti == 0 else []) \
                + (["Ub", "Om", "OmN", "Om2", "Gt", "Gtb"] if ti == 1 else [])
            for c in range(2):
                E("vector", lambda v, c=c, xbf=xbf, xbw=xbw: v.tensor_scalar(
                    xbf[:, c * D:(c + 1) * D], tb[:, c * D:(c + 1) * D],
                    negm1[:, c:c + 1], inv1[:, c:c + 1], op0=ALU.add, op1=ALU.mult),
                  reads=["tb", "negm1", "inv1"], writes=xbw)

        steps.append(s_ln1b)
        return steps


    def emit_x1T(ti):
        sl = ti % 2
        xbf = x1bf[sl]
        xbreg = f"x1bf{sl}"
        xt = x1Tb[ti % 2]
        def x1_tr(t, xbf=xbf):
            lm = None
            for bi in range(16):
                dc, c = bi // 2, bi % 2
                pb = PB[6 + bi // 8]
                lm = t.transpose(pb[:, (bi % 8) * 128:(bi % 8 + 1) * 128],
                                 xbf[:, c * D + dc * 128:c * D + (dc + 1) * 128],
                                 eyeb[:])
            return lm
        E("tensor", x1_tr, reads=[xbreg, "eyeb"], writes=["P6", "P7"])
        xw = [f"x1T{ti % 2}"] + (["scrW1"] if ti == 1 else [])
        E("scalar", lambda s, xt=xt: s.copy(xt[:, 0:1024], PB[6][:, 0:1024]),
          reads=["P6"], writes=xw)
        E("scalar", lambda s, xt=xt: s.copy(xt[:, 1024:2048], PB[7][:, 0:1024]),
          reads=["P7"], writes=xw)

    x1cur = [0]
    def emit_mm1(k):
        hp = P[4 + k % 2]
        hreg = f"P{4 + k % 2}"
        xt = x1Tb[x1cur[0]]
        def mm1(t, k=k, hp=hp, xt=xt):
            lm = None
            for dc in range(DC):
                lm = t.matmul(hp[:, 0:256],
                              W1e[:, dc * H + k * 128:dc * H + (k + 1) * 128],
                              xt[:, dc * 256:(dc + 1) * 256],
                              start=(dc == 0), stop=(dc == DC - 1),
                              skip_group_check=True)
            return lm
        E("tensor", mm1,
          reads=[f"W1k_{k // 8}_{(k % 8) // 4}", f"x1T{x1cur[0]}"], writes=[hreg])
        hb = hbf[k % 2]
        E("scalar", lambda s, k=k, hp=hp, hb=hb: s.activation(
            hb[:], hp[:, 0:256], AF.Gelu, bias=b1col[:, k:k + 1], scale=1.0),
          reads=[hreg, "b1col"], writes=[f"hbf{k % 2}"])

    def emit_mm2(k):
        hb = hbf[k % 2]
        for c in range(2):
            def mm2(t, k=k, hb=hb, c=c):
                lm = None
                for nh in range(2):
                    lm = t.matmul(FP[c][:, nh * 512:(nh + 1) * 512],
                                  hb[:, c * 128:(c + 1) * 128],
                                  W2e[:, k * D + nh * 512:k * D + (nh + 1) * 512],
                                  start=(k == 0), stop=(k == HC - 1),
                                  skip_group_check=True)
                return lm
            E("tensor", mm2, reads=[f"hbf{k % 2}", f"W2k_{k}"],
              writes=[f"F{c}a", f"F{c}b"])

    pending_out = []
    xbl = [xb[0][:], xb[1][:], scrP[:, 0:2048]]
    for _s in attn_steps(0):
        _s()
    emit_x1T(0)

    E("vector", lambda v: v.tensor_copy(scrP[0:1, 0:1], scrP[0:1, 0:1]),
      reads=["scrW0", "scrW1", "ppad", "ppadB", "ppadC", "hmr", "rsn"],
      writes=["scrB"])
    for gi, (eng, sem, lvl, l0, l1) in enumerate(
            [("sync", "rows", 0, 1, 7), ("scalar", "rowsb", 1, 7, 13),
             ("sync", "rowsg", 2, 13, 17)]):
        npair = (l1 - l0) // 2
        view = scrPb[32 * lvl:32 * lvl + 1, 0:npair * 2048].rearrange(
            "q (p c) -> q p c", c=2048)
        lo = l0
        dma(eng, sem, view[:, :, 0:1024],
            x1bf[0][lo // 2:lo // 2 + npair, D:2 * D],
            reads=["x1bf0", "scrB"],
            writes=[f"row{l}" for l in range(l0, l1, 2)])
        le = l0 + 1
        dma(eng, sem, view[:, :, 1024:2048],
            x1bf[0][le // 2:le // 2 + npair, 0:D],
            reads=["x1bf0", "scrB"],
            writes=[f"row{l}" for l in range(l0 + 1, l1, 2)])
    for l in range(2, 17, 2):
        E("vector", lambda v, l=l: v.tensor_scalar(rowap(l), rowap(l),
                                                   mixr[0:1, 2:3], None, op0=ALU.mult),
          reads=[f"row{l}", "mixr"], writes=[f"row{l}"])


    FT = [(F01, "F0a", "F0b"), (F23, "F1a", "F1b")]
    _wu = [0]
    _late = [False]

    def w1_pair(i, dc):
        n = _wu[0]
        _wu[0] += 1
        col = dc * H + i * D
        if _late[0]:
            pr = [(P[4], "P4"), (P[5], "P5")] if n % 2 == 0 else \
                 [(P[6], "P6"), (P[7], "P7")]
            def mms(t, i=i, dc=dc, pr=pr):
                lm = None
                for half in range(2):
                    lm = t.matmul(pr[half][0][:, 0:512],
                                  rowap(2 + 2 * i)[0:1, dc * 128:(dc + 1) * 128],
                                  rowap(1 + 2 * i)[0:1, half * 512:(half + 1) * 512],
                                  start=True, stop=True, skip_group_check=True)
                return lm
            E("tensor", mms,
              reads=[f"row{2 + 2 * i}", f"row{1 + 2 * i}"],
              writes=[pr[0][1], pr[1][1]])
            haps = [pr[0][0][:, 0:512], pr[1][0][:, 0:512]]
            regs = [[pr[0][1]], [pr[1][1]]]
        else:
            fp, ra, rb = FT[n % 2]
            def mms(t, i=i, dc=dc, fp=fp):
                lm = None
                for half in range(2):
                    lm = t.matmul(fp[:, half * 512:(half + 1) * 512],
                                  rowap(2 + 2 * i)[0:1, dc * 128:(dc + 1) * 128],
                                  rowap(1 + 2 * i)[0:1, half * 512:(half + 1) * 512],
                                  start=True, stop=True, skip_group_check=True)
                return lm
            E("tensor", mms,
              reads=[f"row{2 + 2 * i}", f"row{1 + 2 * i}"],
              writes=[ra, rb])
            haps = [fp[:, 0:512], fp[:, 512:1024]]
            regs = [[ra], [rb]]
        for half in range(2):
            E("vector", lambda v, hap=haps[half], col=col, half=half: v.tensor_tensor(
                W1e[:, col + half * 512:col + (half + 1) * 512], hap,
                W1e[:, col + half * 512:col + (half + 1) * 512], op=ALU.add),
              reads=regs[half] + [f"W1d{dc}"], writes=[f"W1k_{i}_{half}"])

    def w2_quad(hc):
        n = _wu[0]
        _wu[0] += 1
        j = hc // 8
        col = hc * D
        if _late[0]:
            pr = [(P[4], "P4"), (P[5], "P5")] if n % 2 == 0 else \
                 [(P[6], "P6"), (P[7], "P7")]
            def qap(i, pr=pr):
                return pr[i // 2][0][:, (i % 2) * 256:(i % 2 + 1) * 256]
            wregs = [pr[0][1], pr[1][1]]
            haps = [pr[0][0][:, 0:512], pr[1][0][:, 0:512]]
            hregs = [[pr[0][1]], [pr[1][1]]]
        else:
            fp, ra, rb = FT[n % 2]
            def qap(i, fp=fp):
                return fp[:, i * 256:(i + 1) * 256]
            wregs = [ra, rb]
            haps = [fp[:, 0:512], fp[:, 512:1024]]
            hregs = [[ra], [rb]]
        eyed = False
        def mms(t, hc=hc, j=j, col=col, qap=qap, eyed=eyed):
            lm = None
            for i in range(4):
                evs = rowap(9 + 2 * i).rearrange("p (q j) -> p q j", j=4)[:, :, j]
                if eyed:
                    t.matmul(qap(i), eyeb[:], W2e[:, col + i * 256:col + (i + 1) * 256],
                             start=(i % 2 == 0), stop=False, skip_group_check=True)
                    lm = t.matmul(qap(i),
                                  rowap(10 + 2 * i)[0:1,
                                                    (hc % 8) * 128:(hc % 8 + 1) * 128],
                                  evs, start=False, stop=True, skip_group_check=True)
                else:
                    lm = t.matmul(qap(i),
                                  rowap(10 + 2 * i)[0:1,
                                                    (hc % 8) * 128:(hc % 8 + 1) * 128],
                                  evs, start=(i % 2 == 0), stop=(i % 2 == 1),
                                  skip_group_check=True)
            return lm
        E("tensor", mms,
          reads=([f"row{9 + 2 * i}" for i in range(4)]
                 + [f"row{10 + 2 * i}" for i in range(4)]
                 + ([f"W2d{hc // 4}", "eyeb"] if eyed else [])),
          writes=wregs)
        for half in range(2):
            if eyed:
                E("scalar", lambda s, hap=haps[half], col=col, half=half: s.copy(
                    W2e[:, col + half * 512:col + (half + 1) * 512], hap),
                  reads=hregs[half], writes=[f"W2k_{hc}"])
            else:
                E("vector", lambda v, hap=haps[half], col=col, half=half: v.tensor_tensor(
                    W2e[:, col + half * 512:col + (half + 1) * 512], hap,
                    W2e[:, col + half * 512:col + (half + 1) * 512], op=ALU.add),
                  reads=hregs[half] + [f"W2d{hc // 4}"], writes=[f"W2k_{hc}"])

    for g in range(8):
        _late[0] = g >= 4
        gi, ghalf = g // 2, g % 2
        if ghalf == 0:
            for dc in range(DC):
                w1_pair(gi, dc)
        for hc in range(4 * g, 4 * g + 4):
            w2_quad(hc)

    for nh in range(2):
        E("tensor", lambda t, nh=nh: t.matmul(
            P[6 + nh][:, 0:512], ones_r[:], brow[0:1, nh * 512:(nh + 1) * 512],
            start=True, stop=True, skip_group_check=True),
          reads=["ones_r", "b2row"] + [f"row{l}" for l in range(1, 17)],
          writes=[f"P{6 + nh}"])
        E("vector", lambda v, nh=nh: v.tensor_copy(
            b2_bc[:, nh * 512:(nh + 1) * 512], P[6 + nh][:, 0:512]),
          reads=[f"P{6 + nh}"], writes=["b2bc"])
    for c in range(2):
        E("gpsimd", lambda g, c=c: g.tensor_tensor(
            x1bf[0][:, c * D:(c + 1) * D], x1bf[0][:, c * D:(c + 1) * D],
            b2_bc[:], op=ALU.add),
          reads=["b2bc", "x1bf0"], writes=["x1bf0"])


    for ti in range(NT):
        x1cur[0] = ti % 2
        sl = ti % 3
        xcur = xbl[sl]
        xreg = f"xb{sl}"
        xbf = x1bf[ti % 2]
        xbreg = f"x1bf{ti % 2}"
        if ti + 1 < NT:
            nsl = (ti + 1) % 3
            nxt = x_h.rearrange("(t p s) d -> t p (s d)", p=128, s=2)[ti + 1]
            extra = (["scrB", "ppad", "ppadB", "ppadC", "rsn", "womb_sb"]
                     + [f"row{l}" for l in range(1, 17)]) if ti + 1 == 2 else []
            dma("sync", "xinA" if nsl == 0 else ("xinB" if nsl == 1 else "xinC"),
                xbl[nsl], nxt,
                writes=[f"xb{nsl}"] + extra)
            for dst, srcap, xr in pending_out:
                dma("sync", "dout", dst, srcap, reads=[xr])
            pending_out.clear()
        emit_mm1(0)
        emit_mm1(1)
        steps = attn_steps(ti + 1) if ti + 1 < NT else []
        for k in range(HC):
            emit_mm2(k)
            if k + 2 < HC:
                emit_mm1(k + 2)
            if steps and 8 <= k < 8 + len(steps) - 1:
                steps[k - 8]()
            if steps and k == 24:
                steps[-1]()
            if k == 26 and ti + 1 < NT:
                emit_x1T(ti + 1)
                nbf = x1bf[(ti + 1) % 2]
                nbreg = f"x1bf{(ti + 1) % 2}"
                for c in range(2):
                    E("gpsimd", lambda g, nbf=nbf, c=c: g.tensor_tensor(
                        nbf[:, c * D:(c + 1) * D], nbf[:, c * D:(c + 1) * D],
                        b2_bc[:], op=ALU.add),
                      reads=["b2bc", nbreg], writes=[nbreg])
        for c in range(2):
            E("vector", lambda v, c=c, xcur=xcur, xbf=xbf: v.tensor_tensor(
                xcur[:, c * D:(c + 1) * D], FP[c][:, 0:1024],
                xbf[:, c * D:(c + 1) * D], op=ALU.add),
              reads=[f"F{c}a", f"F{c}b", xbreg], writes=[xreg])
        for c in range(2):
            E("vector", lambda v, c=c, xcur=xcur: v.bn_stats(
                bst2[:, c * 12:c * 12 + 6], xcur[:, c * D:c * D + 512]),
              reads=[xreg], writes=["bst2"])
            E("vector", lambda v, c=c, xcur=xcur: v.bn_stats(
                bst2[:, c * 12 + 6:c * 12 + 12], xcur[:, c * D + 512:(c + 1) * D]),
              reads=[xreg], writes=["bst2"])
            E("vector", lambda v, c=c: v.bn_aggr(agg2[:, c * 2:c * 2 + 2],
                                                 bst2[:, c * 12:c * 12 + 12]),
              reads=["bst2"], writes=["agg2"])
        agg2_m = agg2[:, 0:4].rearrange("p (c s) -> p c s", s=2)[:, :, 0]
        agg2_v = agg2[:, 0:4].rearrange("p (c s) -> p c s", s=2)[:, :, 1]
        E("vector", lambda v: v.tensor_scalar(q2[:], agg2_v, LN_EPS, None, op0=ALU.add),
          reads=["agg2"], writes=["q2"])
        E("vector", lambda v: v.tensor_scalar(negm2[:], agg2_m, -1.0, None, op0=ALU.mult),
          reads=["agg2"], writes=["negm2"])
        emit_newton("q2", q2[:], inv2[:], "inv2", nwi[:, 4:6],
                    nws[:, 0:2], nws[:, 2:4], "nwsc")
        for c in range(2):
            E("vector", lambda v, c=c, xcur=xcur: v.tensor_scalar(
                xcur[:, c * D:(c + 1) * D], xcur[:, c * D:(c + 1) * D],
                negm2[:, c:c + 1], inv2[:, c:c + 1], op0=ALU.add, op1=ALU.mult),
              reads=["negm2", "inv2", xreg], writes=[xreg])
        dst = out_h[ti * TILE:(ti + 1) * TILE, :].rearrange("(p s) d -> p (s d)", s=2)
        pending_out.append((dst, xcur, xreg))


    for dst, srcap, xreg in pending_out:
        dma("sync", "dout", dst, srcap, reads=[xreg])
    pending_out.clear()
    S.wait_only("sync", [("dout", S.cnt.get("dout", 0))])

    sems = {}
    for name in S.cnt:
        sems[name] = es.enter_context(nc.semaphore(name))
    block = es.enter_context(nc.Block())
    with es:
        S.replay(block, sems)
    return nc


_CACHE = {}

_TRIU_R, _TRIU_C = np.triu_indices(OD, k=1)

def _pad_womb(womb):
    out = np.zeros((1, OD * OD), np.float32)
    out[0, _TRIU_R * OD + _TRIU_C] = womb.ravel()
    return out




def kernel(**inputs):
    if "nc" not in _CACHE:
        _CACHE["nc"] = build_nc()
    nc = _CACHE["nc"]

    f32 = np.float32
    shared = dict(
        w1t=np.ascontiguousarray(np.asarray(inputs["L0_ffn_W1"], f32).T),
        w2t=np.ascontiguousarray(np.asarray(inputs["L0_ffn_W2"], f32).T),
        womt=np.ascontiguousarray(np.asarray(inputs["L0_attn_omega_proj_w"], f32).T),
        womb=np.asarray(inputs["L0_attn_omega_proj_b"], f32).reshape(1, NPARAMS),
        ooutt=np.ascontiguousarray(np.asarray(inputs["L0_attn_omega_out_w"], f32).T),
        oob=np.asarray(inputs["L0_attn_omega_out_b"], f32).reshape(1, D),
        vint=np.ascontiguousarray(np.asarray(inputs["L0_attn_vq_proj_in_w"], f32).T),
        vib=np.asarray(inputs["L0_attn_vq_proj_in_b"], f32).reshape(OD, 1),
        voutt=np.ascontiguousarray(np.asarray(inputs["L0_attn_vq_proj_out_w"], f32).T),
        vob=np.asarray(inputs["L0_attn_vq_proj_out_b"], f32).reshape(1, D),
        cb=np.ascontiguousarray(np.asarray(inputs["L0_attn_codebook"], f32)),
        cbt=np.ascontiguousarray(np.asarray(inputs["L0_attn_codebook"], f32).T),
        nw=np.asarray(inputs["L0_attn_norm_w"], f32).reshape(1, D),
        b1=np.asarray(inputs["L0_ffn_b1"], f32).reshape(1, H),
        b2=np.asarray(inputs["L0_ffn_b2"], f32).reshape(1, D),
        mixes=np.array([[float(np.asarray(inputs["L0_attn_omega_mix"]))],
                        [float(np.asarray(inputs["L0_attn_vq_mix"]))],
                        [float(np.asarray(inputs["L0_ffn_scale"]))]], dtype=f32),
        eye=np.eye(128, dtype=f32),
    )
    X = np.asarray(inputs["L0_attn_x"], f32)
    in_maps = []
    for b in range(B):
        m = dict(shared)
        m["x"] = np.ascontiguousarray(X[b])
        in_maps.append(m)

    res = run_bass_kernel_spmd(nc, in_maps, core_ids=list(range(B)))
    out = np.stack([np.asarray(res.results[b]["out"]) for b in range(B)], axis=0)
    return out.astype(np.float32)


# version 74
# speedup vs baseline: 1.2222x; 1.2222x over previous
import numpy as np
from contextlib import ExitStack

import concourse.bass as bass
import concourse.mybir as mybir
from concourse.bass_utils import run_bass_kernel_spmd

F32 = mybir.dt.float32
BF16 = mybir.dt.bfloat16
F16 = mybir.dt.float16
I32 = mybir.dt.int32
AF = mybir.ActivationFunctionType
ALU = mybir.AluOpType
AX = mybir.AxisListType

B, L, D, H = 8, 8192, 1024, 4096
OD = 64
NPARAMS = OD * (OD - 1) // 2
TILE = 256
NT = L // TILE
DC = D // 128
HC = H // 128
RMS_EPS = 1e-6
LN_EPS = 1e-5
MAGIC = 0x5F3759DF
ROW_OFF = np.concatenate([[0], np.cumsum(OD - 1 - np.arange(OD))]).astype(int)


class Region:
    __slots__ = ("w", "rs")
    def __init__(self):
        self.w = None
        self.rs = []


class Sched:
    ESEM = {"tensor": "pe", "vector": "dve", "scalar": "act", "gpsimd": "pool"}
    ENGS = ["sync", "tensor", "vector", "scalar", "gpsimd"]

    def __init__(self, serial_all=False):
        self.cnt = {}
        self.prog = {e: [] for e in self.ENGS}
        self.waited = {e: {} for e in self.ENGS}
        self.burst = set()
        self.serial = set()
        self.regions = {}
        self.serial_all = serial_all
        self.last = None

    def R(self, name):
        if name not in self.regions:
            self.regions[name] = Region()
        return self.regions[name]

    def _need(self, eng, waits, dep):
        sem, val = dep
        if sem in self.burst:
            val = self.cnt[sem]
        if sem == self.ESEM.get(eng) and sem == "pe":
            return
        if self.waited[eng].get(sem, 0) >= val:
            return
        waits.append((sem, val))
        self.waited[eng][sem] = val

    def emit(self, eng, thunk, reads=(), writes=(), dma_sem=None):
        if dma_sem is not None:
            sem, inc = dma_sem, 16
        else:
            sem, inc = self.ESEM[eng], 1
        self.cnt.setdefault(sem, 0)
        waits = []
        if self.serial_all and self.last is not None:
            leng, lsem, lcnt = self.last
            if leng != eng or lsem != sem:
                self._need(eng, waits, (lsem, lcnt))
        if dma_sem is not None and sem in self.serial and self.cnt[sem] > 0:
            self._need(eng, waits, (sem, self.cnt[sem]))
        for r in reads:
            reg = self.R(r)
            if reg.w:
                self._need(eng, waits, reg.w)
        for w in writes:
            reg = self.R(w)
            if reg.w:
                self._need(eng, waits, reg.w)
            for rd in reg.rs:
                self._need(eng, waits, rd)
        self.cnt[sem] += inc
        cnt = self.cnt[sem]
        for r in reads:
            self.R(r).rs.append((sem, cnt))
        for w in writes:
            reg = self.R(w)
            reg.w = (sem, cnt)
            reg.rs = []
        self.prog[eng].append((waits, thunk, sem, inc))
        self.last = (eng, sem, cnt)

    def wait_only(self, eng, deps):
        self.prog[eng].append(([tuple(d) for d in deps], None, None, 0))

    def replay(self, block, sems):
        for eng in self.ENGS:
            items = self.prog[eng]
            if not items:
                continue
            def body(h, items=items):
                for waits, thunk, sem, inc in items:
                    for (s, v) in waits:
                        h.wait_ge(sems[s], v)
                    if thunk is not None:
                        thunk(h).then_inc(sems[sem], inc)
            getattr(block, eng)(body)


def build_nc():
    nc = bass.Bass("TRN2", target_bir_lowering=False)

    def din(name, shape):
        return nc.dram_tensor(name, list(shape), F32, kind="ExternalInput").ap()

    x_h = din("x", (L, D))
    w1t_h = din("w1t", (D, H))
    w2t_h = din("w2t", (H, D))
    womt_h = din("womt", (D, NPARAMS))
    womb_h = din("womb", (1, NPARAMS))
    ooutt_h = din("ooutt", (OD, D))
    oob_h = din("oob", (1, D))
    vint_h = din("vint", (D, OD))
    vib_h = din("vib", (OD, 1))
    voutt_h = din("voutt", (OD, D))
    vob_h = din("vob", (1, D))
    cb_h = din("cb", (256, OD))
    cbt_h = din("cbt", (OD, 256))
    nw_h = din("nw", (1, D))
    b1_h = din("b1", (1, H))
    b2_h = din("b2", (1, D))
    mix_h = din("mixes", (3, 1))
    eye_h = din("eye", (128, 128))
    out_h = nc.dram_tensor("out", [L, D], F32, kind="ExternalOutput").ap()

    es = ExitStack()
    def sb(name, shape, dt=F32):
        return es.enter_context(nc.sbuf_tensor(name, shape, dt))

    W1e = sb("W1e", [128, DC * H], F16)
    W2e = sb("W2e", [128, HC * D], F16)
    xb = [sb(f"xb{i}", [128, 2 * D]) for i in range(2)]
    tb = sb("tb", [128, 2 * D])
    womb_sb = tb[0:1, 0:NPARAMS]
    x1bf = [sb(f"x1bf{i}", [128, 2 * D], F16) for i in range(2)]
    x1Tb = [sb("x1T0", [128, 2 * D], F16),
            sb("x1T1", [128, 2 * D], F16)]
    hbf = [sb(f"hbf{i}", [128, TILE], F16) for i in range(2)]
    nw_bc = sb("nw_bc", [128, D])
    brow = sb("brow", [1, 2 * D])
    scrP = sb("scrP", [128, 3072])
    scrPb = scrP.ap().bitcast(F16)
    b2_bc = scrPb[:, 4096:5120]
    Mp = sb("Mp", [65, D], F16)
    ovt = sb("ovt", [64, D], F16)
    vout_s = sb("vout_s", [64, D], F16)
    vint_s = sb("vint_s", [128, OD * DC])
    cb_s = sb("cb_s", [128, 2 * OD], F16)
    cbt_s = sb("cbt_s", [64, 256])
    cb2 = sb("cb2", [64, 256])
    eyef = sb("eyef", [128, 128])
    eyeb = sb("eyeb", [128, 128], F16)
    b1col = sb("b1col", [128, HC])
    xhT = sb("xhT", [65, TILE], F16)
    ones16 = sb("ones16", [1, 128], F16)
    bst2 = sb("bst2", [128, 24])
    agg2 = sb("agg2", [128, 4])
    xh_bf = [sb(f"xh_bf{i}", [128, OD], F16) for i in range(2)]
    ones_r = sb("ones_r", [1, 128])
    ones_c = sb("ones_c", [128, 1])
    ones64 = sb("ones64", [64, 1])
    mixs = sb("mixs", [3, 1])
    mixr = sb("mixr", [1, 3])
    omx = sb("omx", [64, 1])
    hm_col = sb("hm_col", [128, DC])
    vqi = sb("vqi", [64, 1])
    code_c = sb("code_c", [64, 1], F16)
    mask_cf = sb("mask_cf", [128, 2], F16)
    sc_n = sb("sc_n", [1, 256])
    sc_mx = sb("sc_mx", [1, 1])
    mask_r = scrP[0:1, 1792:2048]
    mask_rb = sb("mask_rb", [1, 256], F16)
    _sp64 = x1bf[1].ap().bitcast(F32)
    Ub = _sp64[0:64, 0:64]
    Om = _sp64[0:64, 64:128]
    OmN = _sp64[0:64, 128:192]
    Om2 = _sp64[0:64, 192:256]
    Gt = _sp64[0:64, 256:320]
    Gtb = x1bf[1][0:64, 640:704]
    bst = sb("bst", [128, 24])
    agg = sb("agg", [128, 4])
    _st = sb("stat8", [128, 16])
    q_r = _st[:, 0:2]; rstd_r = _st[:, 2:4]
    q1 = _st[:, 4:6]; inv1 = _st[:, 6:8]; negm1 = _st[:, 8:10]
    q2 = _st[:, 10:12]; inv2 = _st[:, 12:14]; negm2 = _st[:, 14:16]
    nwi = sb("nwi", [128, 6], I32)
    nws = sb("nws", [128, 4])

    F01 = es.enter_context(nc.psum_tensor("F01", [128, 1024], F32))
    F23 = es.enter_context(nc.psum_tensor("F23", [128, 1024], F32))
    FP = [F01, F23]
    P = [None, None, None, None] + [
        es.enter_context(nc.psum_tensor(f"P{i}", [128, 512], F32)) for i in range(4, 8)]
    PB = [None, None, None, None] + [p.ap().bitcast(F16) for p in P[4:]]

    import os
    S = Sched(serial_all=bool(int(os.environ.get("KSERIAL", "0"))))
    S.serial.update({"dout"})
    S.burst.update({"xinA", "xinB", "xinC", "xinS", "cwA", "cwB",
                    "wload", "cload",
                    "scat", "rows", "rowsb", "rowsg"})
    E = S.emit

    def dma(eng, sem, dst, src, reads=(), writes=(), **kw):
        E(eng, lambda h, d=dst, s=src: h.dma_start(d, s, **kw),
          reads=reads, writes=writes, dma_sem=sem)

    hm_row = scrP[0:1, 0:D]
    rsn = scrP[0:1, 1280:1536]
    ni = scrP.ap().bitcast(I32)[0:1, 1536:1792]
    W2e32 = W2e.ap().bitcast(F32)
    def womt_c(ci):
        return W2e32[:, ci * 1008:(ci + 1) * 1008]
    PAIR_LVL = {0: (0, 0), 1: (0, 1), 2: (0, 2), 3: (1, 0), 4: (1, 1),
                5: (1, 2), 6: (2, 0), 7: (2, 1)}
    def rowap(l):
        if l % 2 == 1:
            pi, which = (l - 1) // 2, 0
        else:
            pi, which = (l - 2) // 2, 1
        lvl, si = PAIR_LVL[pi]
        slot = si * 2 + which
        return scrPb[32 * lvl:32 * lvl + 1, slot * 1024:(slot + 1) * 1024]

    E("vector", lambda v: v.memset(ones_c[:], 1.0), writes=["ones_c"])
    NPA = L // 256
    paslot = [(xb[0][:], "xb0"), (xb[1][:], "xb1")]
    for ti in range(NPA):
        sl = ti % 2
        xbuf, xbr = paslot[sl]
        src = x_h.rearrange("(t p s) d -> t p (s d)", p=128, s=2)[ti]
        dma(["sync", "scalar"][sl], ["xinA", "xinS"][sl],
            xbuf, src, writes=[xbr])
        pr = (ti // 2) % 2
        acc = scrP[:, pr * D:(pr + 1) * D]
        if ti % 2 == 0:
            E("vector", lambda v, xbuf=xbuf, xbr=xbr, acc=acc: v.tensor_tensor(
                acc, xbuf[:, 0:D], xbuf[:, D:2 * D], op=ALU.add),
              reads=[xbr], writes=[f"paS{pr}"])
        else:
            for hh in range(2):
                E("vector", lambda v, xbuf=xbuf, xbr=xbr, acc=acc, hh=hh:
                  v.tensor_tensor(acc, acc, xbuf[:, hh * D:(hh + 1) * D],
                                  op=ALU.add),
                  reads=[xbr, f"paS{pr}"], writes=[f"paS{pr}"])
            def pa(t, acc=acc, first=(ti == 1), last=(ti == NPA - 1)):
                lm = None
                for nh in range(2):
                    lm = t.matmul(F01[0:1, nh * 512:(nh + 1) * 512], ones_c[:],
                                  acc[:, nh * 512:(nh + 1) * 512],
                                  start=first, stop=last,
                                  skip_group_check=True)
                return lm
            E("tensor", pa, reads=[f"paS{pr}", "ones_c"], writes=["F0a", "F0b"])

    dma("sync", "cload", eyef[:], eye_h, writes=["eyef"])
    dma("sync", "cload", nw_bc[0:1, :], nw_h, writes=["nw_pre"])
    dma("sync", "cload", brow[0:1, D:2 * D], oob_h, writes=["bias_row"])
    dma("sync", "cload", brow[0:1, 0:D], vob_h, writes=["vob_r"])
    dma("sync", "cload", womb_sb, womb_h, writes=["womb_sb", "tbA"])
    dma("sync", "cload", mixs[:], mix_h, writes=["mixs"])
    dma("sync", "cload", mixr[:], mix_h.rearrange("a b -> b a"), writes=["mixr"])
    dma("sync", "cload", vqi[:], vib_h, writes=["vqi"])
    dma("sync", "cload", cbt_s[:], cbt_h, writes=["cbt_s"])
    dma("sync", "cload", vint_s[:].rearrange("p (c v) -> p c v", v=OD),
        vint_h.rearrange("(c p) v -> p c v", p=128), writes=["vint_s"])
    dma("gpsimd", "wload", b1col[:], b1_h.rearrange("a (k p) -> a p k", p=128)[0],
        writes=["b1col"], allow_slow_non_contiguous=True)
    dma("gpsimd", "wload", ovt[:], ooutt_h, writes=["ovt"])
    dma("gpsimd", "wload", vout_s[:], voutt_h, writes=["vout_s"])
    dma("gpsimd", "wload", cb_s[:].rearrange("p (c v) -> p c v", v=OD),
        cb_h.rearrange("(c p) v -> p c v", p=128), writes=["cb_s"])
    for dc in range(DC):
        src = w1t_h.rearrange("(c p) h -> p c h", p=128)[:, dc:dc + 1, :]
        dma("gpsimd", f"wb{dc}",
            W1e[:, dc * H:(dc + 1) * H].rearrange("p (c h) -> p c h", h=H),
            src, writes=[f"W1d{dc}"])

    E("vector", lambda v: v.memset(ones_r[:], 1.0), writes=["ones_r"])
    E("vector", lambda v: v.memset(ones16[:], 1.0), writes=["ones16"])
    E("vector", lambda v: v.memset(xhT[64:65, :], 1.0), writes=["xhT"])
    E("vector", lambda v: v.memset(ones64[:], 1.0), writes=["ones64"])
    E("vector", lambda v: v.tensor_copy(eyeb[:], eyef[:]), reads=["eyef"], writes=["eyeb"])
    for nh in range(2):
        E("tensor", lambda t, nh=nh: t.matmul(
            P[6 + nh][:, 0:512], ones_r[:], nw_bc[0:1, nh * 512:(nh + 1) * 512],
            start=True, stop=True, skip_group_check=True),
          reads=["ones_r", "nw_pre"], writes=[f"P{6 + nh}"])
        E("vector", lambda v, nh=nh: v.tensor_copy(
            nw_bc[:, nh * 512:(nh + 1) * 512], P[6 + nh][:, 0:512]),
          reads=[f"P{6 + nh}"], writes=["nw_bc", "nw_pre"])

    E("vector", lambda v: v.tensor_scalar(hm_row, F01[0:1, 0:1024], 1.0 / L, None,
                                          op0=ALU.mult),
      reads=["F0a", "F0b"], writes=["hmr", "paS0"])
    def hm_tr(t):
        lm = None
        for dc in range(DC):
            lm = t.transpose(F23[:, dc:dc + 1],
                             hm_row[0:1, dc * 128:(dc + 1) * 128], eyef[0:1, 0:1])
        return lm
    E("tensor", hm_tr, reads=["hmr", "eyef"], writes=["F1a"])
    E("vector", lambda v: v.tensor_copy(hm_col[:], F23[:, 0:DC]),
      reads=["F1a"], writes=["hm_col"])

    def pad_ap(r, c0, ln):
        g = r * 64 + c0
        if r < 16:
            return scrP[0:1, g:g + ln]
        if r < 48:
            return scrP[32:33, g - 1024:g - 1024 + ln]
        return scrP[64:65, g - 3072:g - 3072 + ln]
    E("vector", lambda v: v.memset(scrP[0:1, 0:1024], 0.0),
      writes=["ppad", "paS0"])
    E("vector", lambda v: v.memset(scrP[32:33, 0:2048], 0.0),
      writes=["ppad", "paS0", "paS1"])
    E("vector", lambda v: v.memset(scrP[64:65, 0:1024], 0.0),
      writes=["ppad", "paS0"])
    for jh in range(2):
        for dc in range(DC):
            ci = jh * DC + dc
            src = womt_h.rearrange("(h p) j -> h p j", p=128)[dc][
                :, jh * 1008:(jh + 1) * 1008]
            dma(["sync", "scalar"][ci % 2], ["cwA", "cwB"][ci % 2],
                womt_c(ci), src, writes=[f"wmt{ci}"])
            def pm(t, jh=jh, dc=dc, ci=ci):
                lm = None
                for n in range(2):
                    lm = t.matmul(P[4 + jh * 2 + n][0:1, 0:504],
                                  hm_col[:, dc:dc + 1],
                                  womt_c(ci)[:, n * 504:(n + 1) * 504],
                                  start=(dc == 0), stop=False,
                                  skip_group_check=True)
                return lm
            E("tensor", pm, reads=["hm_col", f"wmt{ci}"],
              writes=[f"P{4 + jh * 2}", f"P{5 + jh * 2}"])
    def pm_bias(t):
        lm = None
        for jq in range(4):
            lm = t.matmul(P[4 + jq][0:1, 0:504], ones_c[0:1, 0:1],
                          womb_sb[0:1, jq * 504:(jq + 1) * 504],
                          start=False, stop=True, skip_group_check=True)
        return lm
    E("tensor", pm_bias, reads=["ones_c", "womb_sb"],
      writes=["P4", "P5", "P6", "P7"])
    for j in range(8):
        src = w2t_h.rearrange("(c p) o -> p c o", p=128)[:, j * 4:(j + 1) * 4, :]
        over = [ci for ci in range(16)
                if ci * 1008 < (j + 1) * 2048 and (ci + 1) * 1008 > j * 2048]
        dma("gpsimd", f"wc{j}",
            W2e[:, j * 4 * D:(j + 1) * 4 * D].rearrange("p (c o) -> p c o", o=D),
            src, writes=[f"W2d{j}"] + [f"wmt{ci}" for ci in over])
    for jq in range(4):
        ja_, jb_ = 504 * jq, 504 * (jq + 1)
        for r in range(OD - 1):
            a = max(ja_, int(ROW_OFF[r])); b = min(jb_, int(ROW_OFF[r + 1]))
            if a >= b:
                continue
            c0 = r + 1 + (a - int(ROW_OFF[r]))
            E("vector", lambda v, r=r, a=a, b=b, c0=c0, jq=jq: v.tensor_copy(
                pad_ap(r, c0, b - a), P[4 + jq][0:1, a - 504 * jq:b - 504 * jq]),
              reads=[f"P{4 + jq}"], writes=["ppad"])

    def u_tr(t):
        lm = None
        for r in range(OD):
            base = 0 if r < 16 else (32 if r < 48 else 64)
            lm = t.transpose(P[5][0:64, r:r + 1], pad_ap(r, 0, 64),
                             ones_c[base:base + 1, 0:1])
        return lm
    E("tensor", u_tr, reads=["ppad", "ppadB", "ppadC", "ones_c"],
      writes=["P5"])
    E("vector", lambda v: v.tensor_copy(Ub[:], P[5][0:64, 0:64]),
      reads=["P5"], writes=["Ub"])
    E("tensor", lambda t: t.transpose(P[5][0:64, 0:64], Ub[:], eyef[0:64, 0:64]),
      reads=["Ub", "eyef"], writes=["P5"])
    E("vector", lambda v: v.tensor_tensor(Om[:], P[5][0:64, 0:64], Ub[:], op=ALU.subtract),
      reads=["Ub", "P5"], writes=["Om"])
    E("vector", lambda v: v.tensor_tensor(OmN[:], Ub[:], P[5][0:64, 0:64], op=ALU.subtract),
      reads=["Ub", "P5"], writes=["OmN"])
    E("tensor", lambda t: t.matmul(P[5][0:64, 0:64], OmN[:], Om[:], start=True,
                                   stop=True, skip_group_check=True),
      reads=["OmN", "Om"], writes=["P5"])
    E("vector", lambda v: v.tensor_copy(Om2[:], P[5][0:64, 0:64]),
      reads=["P5"], writes=["Om2"])
    E("tensor", lambda t: t.matmul(P[5][0:64, 0:64], Om2[:], Om[:], start=True,
                                   stop=True, skip_group_check=True),
      reads=["Om2", "Om"], writes=["P5"])
    E("vector", lambda v: v.tensor_tensor(Gt[:], eyef[0:64, 0:64], Om[:], op=ALU.subtract),
      reads=["eyef", "Om"], writes=["Gt"])
    E("vector", lambda v: v.scalar_tensor_tensor(Gt[:], Om2[:], 0.5, Gt[:],
                                                 op0=ALU.mult, op1=ALU.add),
      reads=["Om2", "Gt"], writes=["Gt"])
    E("vector", lambda v: v.scalar_tensor_tensor(Gt[:], P[5][0:64, 0:64], -1.0 / 6.0,
                                                 Gt[:], op0=ALU.mult, op1=ALU.add),
      reads=["P5", "Gt"], writes=["Gt"])
    E("vector", lambda v: v.tensor_copy(Gtb[:], Gt[:]), reads=["Gt"], writes=["Gtb"])
    E("tensor", lambda t: t.matmul(P[4][0:64, 0:512], Gtb[:], ovt[:, 0:512],
                                   start=True, stop=True, skip_group_check=True),
      reads=["Gtb", "ovt"], writes=["P4"])
    E("tensor", lambda t: t.matmul(P[5][0:64, 0:512], Gtb[:], ovt[:, 512:1024],
                                   start=True, stop=True, skip_group_check=True),
      reads=["Gtb", "ovt"], writes=["P5"])
    E("tensor", lambda t: t.matmul(P[6][0:64, 0:1], ones_r[0:1, 0:64], mixs[0:1, 0:1],
                                   start=True, stop=True, skip_group_check=True),
      reads=["ones_r", "mixs"], writes=["P6"])
    E("vector", lambda v: v.tensor_copy(omx[:], P[6][0:64, 0:1]),
      reads=["P6"], writes=["omx"])
    E("vector", lambda v: v.tensor_scalar(Mp[0:64, 0:512], P[4][0:64, 0:512],
                                          omx[:, 0:1], None, op0=ALU.mult),
      reads=["P4", "omx"], writes=["Mp"])
    E("vector", lambda v: v.tensor_scalar(Mp[0:64, 512:1024], P[5][0:64, 0:512],
                                          omx[:, 0:1], None, op0=ALU.mult),
      reads=["P5", "omx"], writes=["Mp"])

    def vq_mm(t):
        lm = None
        for dc in range(DC):
            lm = t.matmul(P[6][0:64, 0:1], vint_s[:, dc * OD:(dc + 1) * OD],
                          hm_col[:, dc:dc + 1], start=(dc == 0), stop=(dc == DC - 1),
                          skip_group_check=True)
        return lm
    E("tensor", vq_mm, reads=["vint_s", "hm_col"], writes=["P6"])
    E("vector", lambda v: v.tensor_tensor(vqi[:], vqi[:], P[6][0:64, 0:1], op=ALU.add),
      reads=["P6", "vqi"], writes=["vqi"])
    E("vector", lambda v: v.tensor_tensor(cb2[:], cbt_s[:], cbt_s[:], op=ALU.mult),
      reads=["cbt_s"], writes=["cb2"])
    E("tensor", lambda t: t.matmul(P[7][0:1, 0:256], ones64[:], cb2[:], start=True,
                                   stop=True, skip_group_check=True),
      reads=["ones64", "cb2"], writes=["P7"])
    def emit_newton(qrg, q_ap, out_ap, outrg, int_ap, s1_ap, s2_ap, scr_rg):
        E("vector", lambda v: v.tensor_scalar(int_ap, q_ap.bitcast(I32), 1, None,
                                              op0=ALU.arith_shift_right),
          reads=[qrg], writes=[scr_rg])
        E("vector", lambda v: v.tensor_scalar(int_ap, int_ap, -1, MAGIC,
                                              op0=ALU.mult, op1=ALU.add),
          reads=[scr_rg], writes=[scr_rg])
        r = int_ap.bitcast(F32)
        for it in range(2):
            E("vector", lambda v: v.tensor_tensor(s1_ap, r, r, op=ALU.mult),
              reads=[scr_rg], writes=[scr_rg])
            E("vector", lambda v: v.tensor_tensor(s1_ap, s1_ap, q_ap, op=ALU.mult),
              reads=[scr_rg, qrg], writes=[scr_rg])
            E("vector", lambda v: v.tensor_scalar(s2_ap, s1_ap, -0.5, 1.5,
                                                  op0=ALU.mult, op1=ALU.add),
              reads=[scr_rg], writes=[scr_rg])
            dst, drg = (out_ap, outrg) if it == 1 else (r, scr_rg)
            E("vector", lambda v, dst=dst: v.tensor_tensor(dst, r, s2_ap, op=ALU.mult),
              reads=[scr_rg], writes=[drg])
    E("vector", lambda v: v.tensor_copy(rsn, P[7][0:1, 0:256]),
      reads=["P7"], writes=["rsn"])
    emit_newton("rsn", rsn, rsn, "rsn", ni, sc_n[0:1, 0:256],
                mask_r[0:1, 0:256], "nscr")
    E("tensor", lambda t: t.matmul(P[7][0:1, 0:256], vqi[:], cbt_s[:], start=True,
                                   stop=True, skip_group_check=True),
      reads=["vqi", "cbt_s"], writes=["P7"])
    E("vector", lambda v: v.tensor_tensor(sc_n[:], P[7][0:1, 0:256], rsn, op=ALU.mult),
      reads=["P7", "rsn"], writes=["sc_n"])
    E("vector", lambda v: v.reduce_max(sc_mx[:], sc_n[:], axis=AX.X),
      reads=["sc_n"], writes=["sc_mx"])
    E("vector", lambda v: v.tensor_scalar(mask_r[:], sc_n[:], sc_mx[0:1, 0:1], None,
                                          op0=ALU.is_ge),
      reads=["sc_n", "sc_mx"], writes=["mask_r"])
    def mask_tr(t):
        t.transpose(P[6][:, 0:1], mask_r[0:1, 0:128], eyef[0:1, 0:1])
        return t.transpose(P[6][:, 1:2], mask_r[0:1, 128:256], eyef[0:1, 0:1])
    E("tensor", mask_tr, reads=["mask_r", "eyef"], writes=["P6"])
    E("vector", lambda v: v.tensor_copy(mask_cf[:], P[6][:, 0:2]),
      reads=["P6"], writes=["mask_cf"])
    def code_mm(t):
        t.matmul(P[6][0:64, 0:1], cb_s[:, 0:64], mask_cf[:, 0:1], start=True,
                 stop=False, skip_group_check=True)
        return t.matmul(P[6][0:64, 0:1], cb_s[:, 64:128], mask_cf[:, 1:2], start=False,
                        stop=True, skip_group_check=True)
    E("tensor", code_mm, reads=["cb_s", "mask_cf"], writes=["P6"])
    E("vector", lambda v: v.tensor_copy(code_c[:], P[6][0:64, 0:1]),
      reads=["P6"], writes=["code_c"])
    def vqc_mm(t):
        t.matmul(P[6][0:1, 0:512], code_c[:], vout_s[:, 0:512], start=True,
                 stop=True, skip_group_check=True)
        return t.matmul(P[7][0:1, 0:512], code_c[:], vout_s[:, 512:1024], start=True,
                        stop=True, skip_group_check=True)
    E("tensor", vqc_mm, reads=["code_c", "vout_s"], writes=["P6", "P7"])
    bias_row = brow[0:1, D:2 * D]
    E("vector", lambda v: v.tensor_scalar(bias_row, bias_row, mixr[0:1, 0:1], None,
                                          op0=ALU.mult),
      reads=["bias_row", "mixr"], writes=["bias_row"])
    E("vector", lambda v: v.scalar_tensor_tensor(bias_row, brow[0:1, 0:D],
                                                 mixr[0:1, 1:2], bias_row,
                                                 op0=ALU.mult, op1=ALU.add),
      reads=["vob_r", "mixr", "bias_row"], writes=["bias_row"])
    for nh, pk in ((0, 6), (1, 7)):
        E("vector", lambda v, nh=nh, pk=pk: v.scalar_tensor_tensor(
            brow[0:1, D + nh * 512:D + (nh + 1) * 512], P[pk][0:1, 0:512],
            mixr[0:1, 1:2], brow[0:1, D + nh * 512:D + (nh + 1) * 512],
            op0=ALU.mult, op1=ALU.add),
          reads=[f"P{pk}", "mixr", "bias_row"], writes=["bias_row"])
    dma("sync", "cload2", brow[0:1, 0:D], b2_h, writes=["vob_r", "b2row"])
    E("vector", lambda v: v.tensor_copy(Mp[64:65, :], bias_row),
      reads=["bias_row"], writes=["Mp"])

    src0 = x_h.rearrange("(t p s) d -> t p (s d)", p=128, s=2)[0]
    dma("sync", "xinA", xb[0][:], src0, writes=["xb0"])


    def attn_steps(ti):
        sl = ti % 3
        xcur = xbl[sl]
        xreg = f"xb{sl}"
        xbf = x1bf[ti % 2]
        xbreg = f"x1bf{ti % 2}"
        steps = []
        def s_xh():
            for c in range(2):
                E("vector", lambda v, c=c, xcur=xcur: v.tensor_copy(
                    xh_bf[c][:], xcur[:, c * D:c * D + OD]),
                  reads=[xreg], writes=[f"xh_bf{c}"])
            for c in range(2):
                E("tensor", lambda t, c=c: t.transpose(
                    PB[6][0:64, 256 + c * 128:256 + (c + 1) * 128], xh_bf[c][:], eyeb[:]),
                  reads=[f"xh_bf{c}", "eyeb"], writes=["P6"])
            for c in range(2):
                E("scalar", lambda s, c=c: s.copy(
                    xhT[0:64, c * 128:(c + 1) * 128],
                    PB[6][0:64, 256 + c * 128:256 + (c + 1) * 128]),
                  reads=["P6"], writes=["xhT"])

        steps.append(s_xh)
        for c in range(2):
            for nh in range(2):
                def s_lie(c=c, nh=nh):
                    pk = 6 + nh
                    E("tensor", lambda t, c=c, nh=nh, pk=pk: t.matmul(
                        P[pk][:, 0:512], xhT[:, c * 128:(c + 1) * 128],
                        Mp[:, nh * 512:(nh + 1) * 512], start=True, stop=True,
                        skip_group_check=True),
                      reads=["xhT", "Mp"], writes=[f"P{pk}"])
                    E("vector", lambda v, c=c, nh=nh, pk=pk, xcur=xcur: v.tensor_tensor(
                        tb[:, c * D + nh * 512:c * D + (nh + 1) * 512], P[pk][:, 0:512],
                        xcur[:, c * D + nh * 512:c * D + (nh + 1) * 512], op=ALU.add),
                      reads=[f"P{pk}", xreg],
                      writes=["tb"] + (["tbA", "womb_sb"] if ti == 0 else []))
                steps.append(s_lie)
        def s_rms():
            for c in range(2):
                E("vector", lambda v, c=c: v.bn_stats(bst[:, c * 12:c * 12 + 6],
                                                      tb[:, c * D:c * D + 512]),
                  reads=["tb"], writes=["bst"])
                E("vector", lambda v, c=c: v.bn_stats(bst[:, c * 12 + 6:c * 12 + 12],
                                                      tb[:, c * D + 512:(c + 1) * D]),
                  reads=["tb"], writes=["bst"])
                E("vector", lambda v, c=c: v.bn_aggr(agg[:, c * 2:c * 2 + 2],
                                                     bst[:, c * 12:c * 12 + 12]),
                  reads=["bst"], writes=["agg"])
                E("vector", lambda v, c=c: v.tensor_tensor(
                    q_r[:, c:c + 1], agg[:, c * 2:c * 2 + 1], agg[:, c * 2:c * 2 + 1],
                    op=ALU.mult),
                  reads=["agg"], writes=["q_r"])
                E("vector", lambda v, c=c: v.tensor_tensor(
                    q_r[:, c:c + 1], q_r[:, c:c + 1], agg[:, c * 2 + 1:c * 2 + 2],
                    op=ALU.add),
                  reads=["agg", "q_r"], writes=["q_r"])
            E("vector", lambda v: v.tensor_scalar(q_r[:], q_r[:], RMS_EPS, None, op0=ALU.add),
              reads=["q_r"], writes=["q_r"])
            emit_newton("q_r", q_r[:], rstd_r[:], "rstd_r", nwi[:, 0:2],
                        nws[:, 0:2], nws[:, 2:4], "nwsc")
            for c in range(2):
                for nh in range(2):
                    s0 = c * D + nh * 512
                    E("vector", lambda v, c=c, s0=s0, nh=nh: v.scalar_tensor_tensor(
                        tb[:, s0:s0 + 512], tb[:, s0:s0 + 512], rstd_r[:, c:c + 1],
                        nw_bc[:, nh * 512:(nh + 1) * 512], op0=ALU.mult, op1=ALU.mult),
                      reads=["rstd_r", "nw_bc", "tb"], writes=["tb"])
            E("vector", lambda v, xcur=xcur: v.tensor_tensor(tb[:], tb[:], xcur,
                                                             op=ALU.add),
              reads=[xreg, "tb"], writes=["tb"])

        steps.append(s_rms)
        def s_ln1():
            for c in range(2):
                E("vector", lambda v, c=c: v.bn_stats(bst[:, c * 12:c * 12 + 6],
                                                      tb[:, c * D:c * D + 512]),
                  reads=["tb"], writes=["bst"])
                E("vector", lambda v, c=c: v.bn_stats(bst[:, c * 12 + 6:c * 12 + 12],
                                                      tb[:, c * D + 512:(c + 1) * D]),
                  reads=["tb"], writes=["bst"])
                E("vector", lambda v, c=c: v.bn_aggr(agg[:, c * 2:c * 2 + 2],
                                                     bst[:, c * 12:c * 12 + 12]),
                  reads=["bst"], writes=["agg"])
            agg_m = agg[:, 0:4].rearrange("p (c s) -> p c s", s=2)[:, :, 0]
            agg_v = agg[:, 0:4].rearrange("p (c s) -> p c s", s=2)[:, :, 1]
            E("vector", lambda v: v.tensor_scalar(q1[:], agg_v, LN_EPS, None, op0=ALU.add),
              reads=["agg"], writes=["q1"])
            E("vector", lambda v: v.tensor_scalar(negm1[:], agg_m, -1.0, None, op0=ALU.mult),
              reads=["agg"], writes=["negm1"])
            emit_newton("q1", q1[:], inv1[:], "inv1", nwi[:, 2:4],
                        nws[:, 0:2], nws[:, 2:4], "nwsc")

        steps.append(s_ln1)
        def s_ln1b():
            xbw = [xbreg] + (["scrW2"] if ti == 0 else []) \
                + (["Ub", "Om", "OmN", "Om2", "Gt", "Gtb"] if ti == 1 else [])
            for c in range(2):
                E("vector", lambda v, c=c, xbf=xbf, xbw=xbw: v.tensor_scalar(
                    xbf[:, c * D:(c + 1) * D], tb[:, c * D:(c + 1) * D],
                    negm1[:, c:c + 1], inv1[:, c:c + 1], op0=ALU.add, op1=ALU.mult),
                  reads=["tb", "negm1", "inv1"], writes=xbw)

        steps.append(s_ln1b)
        return steps


    def emit_x1T(ti):
        sl = ti % 2
        xbf = x1bf[sl]
        xbreg = f"x1bf{sl}"
        xt = x1Tb[ti % 2]
        def x1_tr(t, xbf=xbf):
            lm = None
            for bi in range(16):
                dc, c = bi // 2, bi % 2
                pb = PB[6 + bi // 8]
                lm = t.transpose(pb[:, (bi % 8) * 128:(bi % 8 + 1) * 128],
                                 xbf[:, c * D + dc * 128:c * D + (dc + 1) * 128],
                                 eyeb[:])
            return lm
        E("tensor", x1_tr, reads=[xbreg, "eyeb"], writes=["P6", "P7"])
        xw = [f"x1T{ti % 2}"] + (["scrW1"] if ti == 1 else [])
        E("scalar", lambda s, xt=xt: s.copy(xt[:, 0:1024], PB[6][:, 0:1024]),
          reads=["P6"], writes=xw)
        E("scalar", lambda s, xt=xt: s.copy(xt[:, 1024:2048], PB[7][:, 0:1024]),
          reads=["P7"], writes=xw)

    x1cur = [0]
    def emit_mm1(k):
        hp = P[4 + k % 2]
        hreg = f"P{4 + k % 2}"
        xt = x1Tb[x1cur[0]]
        def mm1(t, k=k, hp=hp, xt=xt):
            lm = None
            for dc in range(DC):
                lm = t.matmul(hp[:, 0:256],
                              W1e[:, dc * H + k * 128:dc * H + (k + 1) * 128],
                              xt[:, dc * 256:(dc + 1) * 256],
                              start=(dc == 0), stop=(dc == DC - 1),
                              skip_group_check=True)
            return lm
        E("tensor", mm1,
          reads=[f"W1k_{k // 8}_{(k % 8) // 4}", f"x1T{x1cur[0]}"], writes=[hreg])
        hb = hbf[k % 2]
        E("scalar", lambda s, k=k, hp=hp, hb=hb: s.activation(
            hb[:], hp[:, 0:256], AF.Gelu, bias=b1col[:, k:k + 1], scale=1.0),
          reads=[hreg, "b1col"], writes=[f"hbf{k % 2}"])

    def emit_mm2(k):
        hb = hbf[k % 2]
        for c in range(2):
            def mm2(t, k=k, hb=hb, c=c):
                lm = None
                for nh in range(2):
                    lm = t.matmul(FP[c][:, nh * 512:(nh + 1) * 512],
                                  hb[:, c * 128:(c + 1) * 128],
                                  W2e[:, k * D + nh * 512:k * D + (nh + 1) * 512],
                                  start=(k == 0), stop=(k == HC - 1),
                                  skip_group_check=True)
                return lm
            E("tensor", mm2, reads=[f"hbf{k % 2}", f"W2k_{k}"],
              writes=[f"F{c}a", f"F{c}b"])

    pending_out = []
    xbl = [xb[0][:], xb[1][:], scrP[:, 0:2048]]
    for _s in attn_steps(0):
        _s()
    emit_x1T(0)

    E("vector", lambda v: v.tensor_copy(scrP[0:1, 0:1], scrP[0:1, 0:1]),
      reads=["scrW0", "scrW1", "ppad", "ppadB", "ppadC", "hmr", "rsn"],
      writes=["scrB"])
    for gi, (eng, sem, lvl, l0, l1) in enumerate(
            [("sync", "rows", 0, 1, 7), ("scalar", "rowsb", 1, 7, 13),
             ("sync", "rowsg", 2, 13, 17)]):
        npair = (l1 - l0) // 2
        view = scrPb[32 * lvl:32 * lvl + 1, 0:npair * 2048].rearrange(
            "q (p c) -> q p c", c=2048)
        lo = l0
        dma(eng, sem, view[:, :, 0:1024],
            x1bf[0][lo // 2:lo // 2 + npair, D:2 * D],
            reads=["x1bf0", "scrB"],
            writes=[f"row{l}" for l in range(l0, l1, 2)])
        le = l0 + 1
        dma(eng, sem, view[:, :, 1024:2048],
            x1bf[0][le // 2:le // 2 + npair, 0:D],
            reads=["x1bf0", "scrB"],
            writes=[f"row{l}" for l in range(l0 + 1, l1, 2)])
    for l in range(2, 17, 2):
        E("vector", lambda v, l=l: v.tensor_scalar(rowap(l), rowap(l),
                                                   mixr[0:1, 2:3], None, op0=ALU.mult),
          reads=[f"row{l}", "mixr"], writes=[f"row{l}"])


    FT = [(F01, "F0a", "F0b"), (F23, "F1a", "F1b")]
    _wu = [0]
    _late = [False]

    def w1_pair(i, dc):
        n = _wu[0]
        _wu[0] += 1
        col = dc * H + i * D
        if _late[0]:
            pr = [(P[4], "P4"), (P[5], "P5")] if n % 2 == 0 else \
                 [(P[6], "P6"), (P[7], "P7")]
            def mms(t, i=i, dc=dc, pr=pr):
                lm = None
                for half in range(2):
                    lm = t.matmul(pr[half][0][:, 0:512],
                                  rowap(2 + 2 * i)[0:1, dc * 128:(dc + 1) * 128],
                                  rowap(1 + 2 * i)[0:1, half * 512:(half + 1) * 512],
                                  start=True, stop=True, skip_group_check=True)
                return lm
            E("tensor", mms,
              reads=[f"row{2 + 2 * i}", f"row{1 + 2 * i}"],
              writes=[pr[0][1], pr[1][1]])
            haps = [pr[0][0][:, 0:512], pr[1][0][:, 0:512]]
            regs = [[pr[0][1]], [pr[1][1]]]
        else:
            fp, ra, rb = FT[n % 2]
            def mms(t, i=i, dc=dc, fp=fp):
                lm = None
                for half in range(2):
                    lm = t.matmul(fp[:, half * 512:(half + 1) * 512],
                                  rowap(2 + 2 * i)[0:1, dc * 128:(dc + 1) * 128],
                                  rowap(1 + 2 * i)[0:1, half * 512:(half + 1) * 512],
                                  start=True, stop=True, skip_group_check=True)
                return lm
            E("tensor", mms,
              reads=[f"row{2 + 2 * i}", f"row{1 + 2 * i}"],
              writes=[ra, rb])
            haps = [fp[:, 0:512], fp[:, 512:1024]]
            regs = [[ra], [rb]]
        for half in range(2):
            E("vector", lambda v, hap=haps[half], col=col, half=half: v.tensor_tensor(
                W1e[:, col + half * 512:col + (half + 1) * 512], hap,
                W1e[:, col + half * 512:col + (half + 1) * 512], op=ALU.add),
              reads=regs[half] + [f"W1d{dc}"], writes=[f"W1k_{i}_{half}"])

    def w2_quad(hc):
        n = _wu[0]
        _wu[0] += 1
        j = hc // 8
        col = hc * D
        if _late[0]:
            pr = [(P[4], "P4"), (P[5], "P5")] if n % 2 == 0 else \
                 [(P[6], "P6"), (P[7], "P7")]
            def qap(i, pr=pr):
                return pr[i // 2][0][:, (i % 2) * 256:(i % 2 + 1) * 256]
            wregs = [pr[0][1], pr[1][1]]
            haps = [pr[0][0][:, 0:512], pr[1][0][:, 0:512]]
            hregs = [[pr[0][1]], [pr[1][1]]]
        else:
            fp, ra, rb = FT[n % 2]
            def qap(i, fp=fp):
                return fp[:, i * 256:(i + 1) * 256]
            wregs = [ra, rb]
            haps = [fp[:, 0:512], fp[:, 512:1024]]
            hregs = [[ra], [rb]]
        eyed = False
        def mms(t, hc=hc, j=j, col=col, qap=qap, eyed=eyed):
            lm = None
            for i in range(4):
                evs = rowap(9 + 2 * i).rearrange("p (q j) -> p q j", j=4)[:, :, j]
                if eyed:
                    t.matmul(qap(i), eyeb[:], W2e[:, col + i * 256:col + (i + 1) * 256],
                             start=(i % 2 == 0), stop=False, skip_group_check=True)
                    lm = t.matmul(qap(i),
                                  rowap(10 + 2 * i)[0:1,
                                                    (hc % 8) * 128:(hc % 8 + 1) * 128],
                                  evs, start=False, stop=True, skip_group_check=True)
                else:
                    lm = t.matmul(qap(i),
                                  rowap(10 + 2 * i)[0:1,
                                                    (hc % 8) * 128:(hc % 8 + 1) * 128],
                                  evs, start=(i % 2 == 0), stop=(i % 2 == 1),
                                  skip_group_check=True)
            return lm
        E("tensor", mms,
          reads=([f"row{9 + 2 * i}" for i in range(4)]
                 + [f"row{10 + 2 * i}" for i in range(4)]
                 + ([f"W2d{hc // 4}", "eyeb"] if eyed else [])),
          writes=wregs)
        for half in range(2):
            if eyed:
                E("scalar", lambda s, hap=haps[half], col=col, half=half: s.copy(
                    W2e[:, col + half * 512:col + (half + 1) * 512], hap),
                  reads=hregs[half], writes=[f"W2k_{hc}"])
            else:
                E("vector", lambda v, hap=haps[half], col=col, half=half: v.tensor_tensor(
                    W2e[:, col + half * 512:col + (half + 1) * 512], hap,
                    W2e[:, col + half * 512:col + (half + 1) * 512], op=ALU.add),
                  reads=hregs[half] + [f"W2d{hc // 4}"], writes=[f"W2k_{hc}"])

    for g in range(8):
        _late[0] = g >= 4
        gi, ghalf = g // 2, g % 2
        if ghalf == 0:
            for dc in range(DC):
                w1_pair(gi, dc)
        for hc in range(4 * g, 4 * g + 4):
            w2_quad(hc)

    for nh in range(2):
        E("tensor", lambda t, nh=nh: t.matmul(
            P[6 + nh][:, 0:512], ones_r[:], brow[0:1, nh * 512:(nh + 1) * 512],
            start=True, stop=True, skip_group_check=True),
          reads=["ones_r", "b2row"] + [f"row{l}" for l in range(1, 17)],
          writes=[f"P{6 + nh}"])
        E("vector", lambda v, nh=nh: v.tensor_copy(
            b2_bc[:, nh * 512:(nh + 1) * 512], P[6 + nh][:, 0:512]),
          reads=[f"P{6 + nh}"], writes=["b2bc"])
    for c in range(2):
        E("gpsimd", lambda g, c=c: g.tensor_tensor(
            x1bf[0][:, c * D:(c + 1) * D], x1bf[0][:, c * D:(c + 1) * D],
            b2_bc[:], op=ALU.add),
          reads=["b2bc", "x1bf0"], writes=["x1bf0"])


    for ti in range(NT):
        x1cur[0] = ti % 2
        sl = ti % 3
        xcur = xbl[sl]
        xreg = f"xb{sl}"
        xbf = x1bf[ti % 2]
        xbreg = f"x1bf{ti % 2}"
        if ti + 1 < NT:
            nsl = (ti + 1) % 3
            nxt = x_h.rearrange("(t p s) d -> t p (s d)", p=128, s=2)[ti + 1]
            extra = (["scrB", "ppad", "ppadB", "ppadC", "rsn", "womb_sb"]
                     + [f"row{l}" for l in range(1, 17)]) if ti + 1 == 2 else []
            dma("sync", "xinA" if nsl == 0 else ("xinB" if nsl == 1 else "xinC"),
                xbl[nsl], nxt,
                writes=[f"xb{nsl}"] + extra)
            for dst, srcap, xr in pending_out:
                dma("sync", "dout", dst, srcap, reads=[xr])
            pending_out.clear()
        emit_mm1(0)
        emit_mm1(1)
        steps = attn_steps(ti + 1) if ti + 1 < NT else []
        for k in range(HC):
            emit_mm2(k)
            if k + 2 < HC:
                emit_mm1(k + 2)
            if steps and 8 <= k < 8 + len(steps) - 1:
                steps[k - 8]()
            if steps and k == 24:
                steps[-1]()
            if k == 26 and ti + 1 < NT:
                emit_x1T(ti + 1)
                nbf = x1bf[(ti + 1) % 2]
                nbreg = f"x1bf{(ti + 1) % 2}"
                for c in range(2):
                    E("gpsimd", lambda g, nbf=nbf, c=c: g.tensor_tensor(
                        nbf[:, c * D:(c + 1) * D], nbf[:, c * D:(c + 1) * D],
                        b2_bc[:], op=ALU.add),
                      reads=["b2bc", nbreg], writes=[nbreg])
        for c in range(2):
            E("vector", lambda v, c=c, xcur=xcur, xbf=xbf: v.tensor_tensor(
                xcur[:, c * D:(c + 1) * D], FP[c][:, 0:1024],
                xbf[:, c * D:(c + 1) * D], op=ALU.add),
              reads=[f"F{c}a", f"F{c}b", xbreg], writes=[xreg])
        for c in range(2):
            E("vector", lambda v, c=c, xcur=xcur: v.bn_stats(
                bst2[:, c * 12:c * 12 + 6], xcur[:, c * D:c * D + 512]),
              reads=[xreg], writes=["bst2"])
            E("vector", lambda v, c=c, xcur=xcur: v.bn_stats(
                bst2[:, c * 12 + 6:c * 12 + 12], xcur[:, c * D + 512:(c + 1) * D]),
              reads=[xreg], writes=["bst2"])
            E("vector", lambda v, c=c: v.bn_aggr(agg2[:, c * 2:c * 2 + 2],
                                                 bst2[:, c * 12:c * 12 + 12]),
              reads=["bst2"], writes=["agg2"])
        agg2_m = agg2[:, 0:4].rearrange("p (c s) -> p c s", s=2)[:, :, 0]
        agg2_v = agg2[:, 0:4].rearrange("p (c s) -> p c s", s=2)[:, :, 1]
        E("vector", lambda v: v.tensor_scalar(q2[:], agg2_v, LN_EPS, None, op0=ALU.add),
          reads=["agg2"], writes=["q2"])
        E("vector", lambda v: v.tensor_scalar(negm2[:], agg2_m, -1.0, None, op0=ALU.mult),
          reads=["agg2"], writes=["negm2"])
        emit_newton("q2", q2[:], inv2[:], "inv2", nwi[:, 4:6],
                    nws[:, 0:2], nws[:, 2:4], "nwsc")
        for c in range(2):
            E("vector", lambda v, c=c, xcur=xcur: v.tensor_scalar(
                xcur[:, c * D:(c + 1) * D], xcur[:, c * D:(c + 1) * D],
                negm2[:, c:c + 1], inv2[:, c:c + 1], op0=ALU.add, op1=ALU.mult),
              reads=["negm2", "inv2", xreg], writes=[xreg])
        dst = out_h[ti * TILE:(ti + 1) * TILE, :].rearrange("(p s) d -> p (s d)", s=2)
        pending_out.append((dst, xcur, xreg))


    for dst, srcap, xreg in pending_out:
        dma("sync", "dout", dst, srcap, reads=[xreg])
    pending_out.clear()
    S.wait_only("sync", [("dout", S.cnt.get("dout", 0))])

    sems = {}
    for name in S.cnt:
        sems[name] = es.enter_context(nc.semaphore(name))
    block = es.enter_context(nc.Block())
    with es:
        S.replay(block, sems)
    return nc


_CACHE = {}

_TRIU_R, _TRIU_C = np.triu_indices(OD, k=1)

def _pad_womb(womb):
    out = np.zeros((1, OD * OD), np.float32)
    out[0, _TRIU_R * OD + _TRIU_C] = womb.ravel()
    return out




def kernel(**inputs):
    if "nc" not in _CACHE:
        _CACHE["nc"] = build_nc()
    nc = _CACHE["nc"]

    f32 = np.float32
    shared = dict(
        w1t=np.ascontiguousarray(np.asarray(inputs["L0_ffn_W1"], f32).T),
        w2t=np.ascontiguousarray(np.asarray(inputs["L0_ffn_W2"], f32).T),
        womt=np.ascontiguousarray(np.asarray(inputs["L0_attn_omega_proj_w"], f32).T),
        womb=np.asarray(inputs["L0_attn_omega_proj_b"], f32).reshape(1, NPARAMS),
        ooutt=np.ascontiguousarray(np.asarray(inputs["L0_attn_omega_out_w"], f32).T),
        oob=np.asarray(inputs["L0_attn_omega_out_b"], f32).reshape(1, D),
        vint=np.ascontiguousarray(np.asarray(inputs["L0_attn_vq_proj_in_w"], f32).T),
        vib=np.asarray(inputs["L0_attn_vq_proj_in_b"], f32).reshape(OD, 1),
        voutt=np.ascontiguousarray(np.asarray(inputs["L0_attn_vq_proj_out_w"], f32).T),
        vob=np.asarray(inputs["L0_attn_vq_proj_out_b"], f32).reshape(1, D),
        cb=np.ascontiguousarray(np.asarray(inputs["L0_attn_codebook"], f32)),
        cbt=np.ascontiguousarray(np.asarray(inputs["L0_attn_codebook"], f32).T),
        nw=np.asarray(inputs["L0_attn_norm_w"], f32).reshape(1, D),
        b1=np.asarray(inputs["L0_ffn_b1"], f32).reshape(1, H),
        b2=np.asarray(inputs["L0_ffn_b2"], f32).reshape(1, D),
        mixes=np.array([[float(np.asarray(inputs["L0_attn_omega_mix"]))],
                        [float(np.asarray(inputs["L0_attn_vq_mix"]))],
                        [float(np.asarray(inputs["L0_ffn_scale"]))]], dtype=f32),
        eye=np.eye(128, dtype=f32),
    )
    X = np.asarray(inputs["L0_attn_x"], f32)
    in_maps = []
    for b in range(B):
        m = dict(shared)
        m["x"] = np.ascontiguousarray(X[b])
        in_maps.append(m)

    res = run_bass_kernel_spmd(nc, in_maps, core_ids=list(range(B)))
    out = np.stack([np.asarray(res.results[b]["out"]) for b in range(B)], axis=0)
    return out.astype(np.float32)


# version 75
# speedup vs baseline: 1.2274x; 1.0042x over previous
import numpy as np
from contextlib import ExitStack

import concourse.bass as bass
import concourse.mybir as mybir
from concourse.bass_utils import run_bass_kernel_spmd

F32 = mybir.dt.float32
BF16 = mybir.dt.bfloat16
F16 = mybir.dt.float16
I32 = mybir.dt.int32
AF = mybir.ActivationFunctionType
ALU = mybir.AluOpType
AX = mybir.AxisListType

B, L, D, H = 8, 8192, 1024, 4096
OD = 64
NPARAMS = OD * (OD - 1) // 2
TILE = 256
NT = L // TILE
DC = D // 128
HC = H // 128
RMS_EPS = 1e-6
LN_EPS = 1e-5
MAGIC = 0x5F3759DF
ROW_OFF = np.concatenate([[0], np.cumsum(OD - 1 - np.arange(OD))]).astype(int)


class Region:
    __slots__ = ("w", "rs")
    def __init__(self):
        self.w = None
        self.rs = []


class Sched:
    ESEM = {"tensor": "pe", "vector": "dve", "scalar": "act", "gpsimd": "pool"}
    ENGS = ["sync", "tensor", "vector", "scalar", "gpsimd"]

    def __init__(self, serial_all=False):
        self.cnt = {}
        self.prog = {e: [] for e in self.ENGS}
        self.waited = {e: {} for e in self.ENGS}
        self.burst = set()
        self.serial = set()
        self.regions = {}
        self.serial_all = serial_all
        self.last = None

    def R(self, name):
        if name not in self.regions:
            self.regions[name] = Region()
        return self.regions[name]

    def _need(self, eng, waits, dep):
        sem, val = dep
        if sem in self.burst:
            val = self.cnt[sem]
        if sem == self.ESEM.get(eng) and sem == "pe":
            return
        if self.waited[eng].get(sem, 0) >= val:
            return
        waits.append((sem, val))
        self.waited[eng][sem] = val

    def emit(self, eng, thunk, reads=(), writes=(), dma_sem=None):
        if dma_sem is not None:
            sem, inc = dma_sem, 16
        else:
            sem, inc = self.ESEM[eng], 1
        self.cnt.setdefault(sem, 0)
        waits = []
        if self.serial_all and self.last is not None:
            leng, lsem, lcnt = self.last
            if leng != eng or lsem != sem:
                self._need(eng, waits, (lsem, lcnt))
        if dma_sem is not None and sem in self.serial and self.cnt[sem] > 0:
            self._need(eng, waits, (sem, self.cnt[sem]))
        for r in reads:
            reg = self.R(r)
            if reg.w:
                self._need(eng, waits, reg.w)
        for w in writes:
            reg = self.R(w)
            if reg.w:
                self._need(eng, waits, reg.w)
            for rd in reg.rs:
                self._need(eng, waits, rd)
        self.cnt[sem] += inc
        cnt = self.cnt[sem]
        for r in reads:
            self.R(r).rs.append((sem, cnt))
        for w in writes:
            reg = self.R(w)
            reg.w = (sem, cnt)
            reg.rs = []
        self.prog[eng].append((waits, thunk, sem, inc))
        self.last = (eng, sem, cnt)

    def wait_only(self, eng, deps):
        self.prog[eng].append(([tuple(d) for d in deps], None, None, 0))

    def replay(self, block, sems):
        for eng in self.ENGS:
            items = self.prog[eng]
            if not items:
                continue
            def body(h, items=items):
                for waits, thunk, sem, inc in items:
                    for (s, v) in waits:
                        h.wait_ge(sems[s], v)
                    if thunk is not None:
                        thunk(h).then_inc(sems[sem], inc)
            getattr(block, eng)(body)


def build_nc():
    nc = bass.Bass("TRN2", target_bir_lowering=False)

    def din(name, shape):
        return nc.dram_tensor(name, list(shape), F32, kind="ExternalInput").ap()

    x_h = din("x", (L, D))
    w1t_h = din("w1t", (D, H))
    w2t_h = din("w2t", (H, D))
    womt_h = din("womt", (D, NPARAMS))
    womb_h = din("womb", (1, NPARAMS))
    ooutt_h = din("ooutt", (OD, D))
    oob_h = din("oob", (1, D))
    vint_h = din("vint", (D, OD))
    vib_h = din("vib", (OD, 1))
    voutt_h = din("voutt", (OD, D))
    vob_h = din("vob", (1, D))
    cb_h = din("cb", (256, OD))
    cbt_h = din("cbt", (OD, 256))
    nw_h = din("nw", (1, D))
    b1_h = din("b1", (1, H))
    b2_h = din("b2", (1, D))
    mix_h = din("mixes", (3, 1))
    eye_h = din("eye", (128, 128))
    out_h = nc.dram_tensor("out", [L, D], F32, kind="ExternalOutput").ap()

    es = ExitStack()
    def sb(name, shape, dt=F32):
        return es.enter_context(nc.sbuf_tensor(name, shape, dt))

    W1e = sb("W1e", [128, DC * H], F16)
    W2e = sb("W2e", [128, HC * D], F16)
    xb = [sb(f"xb{i}", [128, 2 * D]) for i in range(2)]
    tb = sb("tb", [128, 2 * D])
    womb_sb = tb[0:1, 0:NPARAMS]
    x1bf = [sb(f"x1bf{i}", [128, 2 * D], F16) for i in range(2)]
    x1Tb = [sb("x1T0", [128, 2 * D], F16),
            sb("x1T1", [128, 2 * D], F16)]
    hbf = [sb(f"hbf{i}", [128, TILE], F16) for i in range(2)]
    nw_bc = sb("nw_bc", [128, D])
    brow = sb("brow", [1, 2 * D])
    scrP = sb("scrP", [128, 3072])
    scrPb = scrP.ap().bitcast(F16)
    b2_bc = scrPb[:, 4096:5120]
    Mp = sb("Mp", [65, D], F16)
    ovt = sb("ovt", [64, D], F16)
    vout_s = sb("vout_s", [64, D], F16)
    vint_s = sb("vint_s", [128, OD * DC])
    cb_s = sb("cb_s", [128, 2 * OD], F16)
    cbt_s = sb("cbt_s", [64, 256])
    cb2 = sb("cb2", [64, 256])
    eyef = sb("eyef", [128, 128])
    eyeb = sb("eyeb", [128, 128], F16)
    b1col = sb("b1col", [128, HC])
    xhT = sb("xhT", [65, TILE], F16)
    ones16 = sb("ones16", [1, 128], F16)
    bst2 = sb("bst2", [128, 24])
    agg2 = sb("agg2", [128, 4])
    xh_bf = [sb(f"xh_bf{i}", [128, OD], F16) for i in range(2)]
    ones_r = sb("ones_r", [1, 128])
    ones_c = sb("ones_c", [128, 1])
    ones64 = sb("ones64", [64, 1])
    mixs = sb("mixs", [3, 1])
    mixr = sb("mixr", [1, 3])
    omx = sb("omx", [64, 1])
    hm_col = sb("hm_col", [128, DC])
    vqi = sb("vqi", [64, 1])
    code_c = sb("code_c", [64, 1], F16)
    mask_cf = sb("mask_cf", [128, 2], F16)
    sc_n = sb("sc_n", [1, 256])
    sc_mx = sb("sc_mx", [1, 1])
    mask_r = scrP[0:1, 1792:2048]
    mask_rb = sb("mask_rb", [1, 256], F16)
    _sp64 = x1bf[1].ap().bitcast(F32)
    Ub = _sp64[0:64, 0:64]
    Om = _sp64[0:64, 64:128]
    OmN = _sp64[0:64, 128:192]
    Om2 = _sp64[0:64, 192:256]
    Gt = _sp64[0:64, 256:320]
    Gtb = x1bf[1][0:64, 640:704]
    bst = sb("bst", [128, 24])
    agg = sb("agg", [128, 4])
    _st = sb("stat8", [128, 16])
    q_r = _st[:, 0:2]; rstd_r = _st[:, 2:4]
    q1 = _st[:, 4:6]; inv1 = _st[:, 6:8]; negm1 = _st[:, 8:10]
    q2 = _st[:, 10:12]; inv2 = _st[:, 12:14]; negm2 = _st[:, 14:16]
    nwi = sb("nwi", [128, 6], I32)
    nws = sb("nws", [128, 4])

    F01 = es.enter_context(nc.psum_tensor("F01", [128, 1024], F32))
    F23 = es.enter_context(nc.psum_tensor("F23", [128, 1024], F32))
    FP = [F01, F23]
    P = [None, None, None, None] + [
        es.enter_context(nc.psum_tensor(f"P{i}", [128, 512], F32)) for i in range(4, 8)]
    PB = [None, None, None, None] + [p.ap().bitcast(F16) for p in P[4:]]

    import os
    S = Sched(serial_all=bool(int(os.environ.get("KSERIAL", "0"))))
    S.serial.update({"dout"})
    S.burst.update({"xinA", "xinB", "xinC", "xinS", "cwA", "cwB",
                    "wload", "cload",
                    "scat", "rows", "rowsb", "rowsg"})
    E = S.emit

    def dma(eng, sem, dst, src, reads=(), writes=(), **kw):
        E(eng, lambda h, d=dst, s=src: h.dma_start(d, s, **kw),
          reads=reads, writes=writes, dma_sem=sem)

    hm_row = scrP[0:1, 0:D]
    rsn = scrP[0:1, 1280:1536]
    ni = scrP.ap().bitcast(I32)[0:1, 1536:1792]
    W2e32 = W2e.ap().bitcast(F32)
    def womt_c(ci):
        return W2e32[:, ci * 1008:(ci + 1) * 1008]
    PAIR_LVL = {0: (0, 0), 1: (0, 1), 2: (0, 2), 3: (1, 0), 4: (1, 1),
                5: (1, 2), 6: (2, 0), 7: (2, 1)}
    def rowap(l):
        if l % 2 == 1:
            pi, which = (l - 1) // 2, 0
        else:
            pi, which = (l - 2) // 2, 1
        lvl, si = PAIR_LVL[pi]
        slot = si * 2 + which
        return scrPb[32 * lvl:32 * lvl + 1, slot * 1024:(slot + 1) * 1024]

    E("vector", lambda v: v.memset(ones_c[:], 1.0), writes=["ones_c"])
    NPA = L // 256
    paslot = [(xb[0][:], "xb0"), (xb[1][:], "xb1")]
    for ti in range(NPA):
        sl = ti % 2
        xbuf, xbr = paslot[sl]
        src = x_h.rearrange("(t p s) d -> t p (s d)", p=128, s=2)[ti]
        dma(["sync", "scalar"][sl], ["xinA", "xinS"][sl],
            xbuf, src, writes=[xbr])
        pr = (ti // 2) % 2
        acc = scrP[:, pr * D:(pr + 1) * D]
        if ti % 2 == 0:
            E("vector", lambda v, xbuf=xbuf, xbr=xbr, acc=acc: v.tensor_tensor(
                acc, xbuf[:, 0:D], xbuf[:, D:2 * D], op=ALU.add),
              reads=[xbr], writes=[f"paS{pr}"])
        else:
            for hh in range(2):
                E("vector", lambda v, xbuf=xbuf, xbr=xbr, acc=acc, hh=hh:
                  v.tensor_tensor(acc, acc, xbuf[:, hh * D:(hh + 1) * D],
                                  op=ALU.add),
                  reads=[xbr, f"paS{pr}"], writes=[f"paS{pr}"])
            def pa(t, acc=acc, first=(ti == 1), last=(ti == NPA - 1)):
                lm = None
                for nh in range(2):
                    lm = t.matmul(F01[0:1, nh * 512:(nh + 1) * 512], ones_c[:],
                                  acc[:, nh * 512:(nh + 1) * 512],
                                  start=first, stop=last,
                                  skip_group_check=True)
                return lm
            E("tensor", pa, reads=[f"paS{pr}", "ones_c"], writes=["F0a", "F0b"])

    dma("sync", "cload", eyef[:], eye_h, writes=["eyef"])
    dma("sync", "cload", nw_bc[0:1, :], nw_h, writes=["nw_pre"])
    dma("sync", "cload", brow[0:1, D:2 * D], oob_h, writes=["bias_row"])
    dma("sync", "cload", brow[0:1, 0:D], vob_h, writes=["vob_r"])
    dma("sync", "cload", womb_sb, womb_h, writes=["womb_sb", "tbA"])
    dma("sync", "cload", mixs[:], mix_h, writes=["mixs"])
    dma("sync", "cload", mixr[:], mix_h.rearrange("a b -> b a"), writes=["mixr"])
    dma("sync", "cload", vqi[:], vib_h, writes=["vqi"])
    dma("sync", "cload", cbt_s[:], cbt_h, writes=["cbt_s"])
    dma("sync", "cload", vint_s[:].rearrange("p (c v) -> p c v", v=OD),
        vint_h.rearrange("(c p) v -> p c v", p=128), writes=["vint_s"])
    dma("gpsimd", "wload", b1col[:], b1_h.rearrange("a (k p) -> a p k", p=128)[0],
        writes=["b1col"], allow_slow_non_contiguous=True)
    dma("gpsimd", "wload", ovt[:], ooutt_h, writes=["ovt"])
    dma("gpsimd", "wload", vout_s[:], voutt_h, writes=["vout_s"])
    dma("gpsimd", "wload", cb_s[:].rearrange("p (c v) -> p c v", v=OD),
        cb_h.rearrange("(c p) v -> p c v", p=128), writes=["cb_s"])
    for dc in range(DC):
        src = w1t_h.rearrange("(c p) h -> p c h", p=128)[:, dc:dc + 1, :]
        dma("gpsimd", f"wb{dc}",
            W1e[:, dc * H:(dc + 1) * H].rearrange("p (c h) -> p c h", h=H),
            src, writes=[f"W1d{dc}"])

    E("vector", lambda v: v.memset(ones_r[:], 1.0), writes=["ones_r"])
    E("vector", lambda v: v.memset(ones16[:], 1.0), writes=["ones16"])
    E("vector", lambda v: v.memset(xhT[64:65, :], 1.0), writes=["xhT"])
    E("vector", lambda v: v.memset(ones64[:], 1.0), writes=["ones64"])
    E("vector", lambda v: v.tensor_copy(eyeb[:], eyef[:]), reads=["eyef"], writes=["eyeb"])
    for nh in range(2):
        E("tensor", lambda t, nh=nh: t.matmul(
            P[6 + nh][:, 0:512], ones_r[:], nw_bc[0:1, nh * 512:(nh + 1) * 512],
            start=True, stop=True, skip_group_check=True),
          reads=["ones_r", "nw_pre"], writes=[f"P{6 + nh}"])
        E("vector", lambda v, nh=nh: v.tensor_copy(
            nw_bc[:, nh * 512:(nh + 1) * 512], P[6 + nh][:, 0:512]),
          reads=[f"P{6 + nh}"], writes=["nw_bc", "nw_pre"])

    E("vector", lambda v: v.tensor_scalar(hm_row, F01[0:1, 0:1024], 1.0 / L, None,
                                          op0=ALU.mult),
      reads=["F0a", "F0b"], writes=["hmr", "paS0"])
    def hm_tr(t):
        lm = None
        for dc in range(DC):
            lm = t.transpose(F23[:, dc:dc + 1],
                             hm_row[0:1, dc * 128:(dc + 1) * 128], eyef[0:1, 0:1])
        return lm
    E("tensor", hm_tr, reads=["hmr", "eyef"], writes=["F1a"])
    E("vector", lambda v: v.tensor_copy(hm_col[:], F23[:, 0:DC]),
      reads=["F1a"], writes=["hm_col"])

    def pad_ap(r, c0, ln):
        g = r * 64 + c0
        if r < 16:
            return scrP[0:1, g:g + ln]
        if r < 48:
            return scrP[32:33, g - 1024:g - 1024 + ln]
        return scrP[64:65, g - 3072:g - 3072 + ln]
    E("vector", lambda v: v.memset(scrP[0:1, 0:1024], 0.0),
      writes=["ppad", "paS0"])
    E("vector", lambda v: v.memset(scrP[32:33, 0:2048], 0.0),
      writes=["ppad", "paS0", "paS1"])
    E("vector", lambda v: v.memset(scrP[64:65, 0:1024], 0.0),
      writes=["ppad", "paS0"])
    for jh in range(2):
        for dc in range(DC):
            ci = jh * DC + dc
            src = womt_h.rearrange("(h p) j -> h p j", p=128)[dc][
                :, jh * 1008:(jh + 1) * 1008]
            dma(["sync", "scalar"][ci % 2], ["cwA", "cwB"][ci % 2],
                womt_c(ci), src, writes=[f"wmt{ci}"])
            def pm(t, jh=jh, dc=dc, ci=ci):
                lm = None
                for n in range(2):
                    lm = t.matmul(P[4 + jh * 2 + n][0:1, 0:504],
                                  hm_col[:, dc:dc + 1],
                                  womt_c(ci)[:, n * 504:(n + 1) * 504],
                                  start=(dc == 0), stop=False,
                                  skip_group_check=True)
                return lm
            E("tensor", pm, reads=["hm_col", f"wmt{ci}"],
              writes=[f"P{4 + jh * 2}", f"P{5 + jh * 2}"])
        def pm_bias(t, jh=jh):
            lm = None
            for jq in (2 * jh, 2 * jh + 1):
                lm = t.matmul(P[4 + jq][0:1, 0:504], ones_c[0:1, 0:1],
                              womb_sb[0:1, jq * 504:(jq + 1) * 504],
                              start=False, stop=True, skip_group_check=True)
            return lm
        E("tensor", pm_bias, reads=["ones_c", "womb_sb"],
          writes=[f"P{4 + 2 * jh}", f"P{5 + 2 * jh}"])
        for jq in (2 * jh, 2 * jh + 1):
            ja_, jb_ = 504 * jq, 504 * (jq + 1)
            for r in range(OD - 1):
                a = max(ja_, int(ROW_OFF[r])); b = min(jb_, int(ROW_OFF[r + 1]))
                if a >= b:
                    continue
                c0 = r + 1 + (a - int(ROW_OFF[r]))
                E("vector", lambda v, r=r, a=a, b=b, c0=c0, jq=jq: v.tensor_copy(
                    pad_ap(r, c0, b - a), P[4 + jq][0:1, a - 504 * jq:b - 504 * jq]),
                  reads=[f"P{4 + jq}"], writes=["ppad"])
    for j in range(8):
        src = w2t_h.rearrange("(c p) o -> p c o", p=128)[:, j * 4:(j + 1) * 4, :]
        over = [ci for ci in range(16)
                if ci * 1008 < (j + 1) * 2048 and (ci + 1) * 1008 > j * 2048]
        dma("gpsimd", f"wc{j}",
            W2e[:, j * 4 * D:(j + 1) * 4 * D].rearrange("p (c o) -> p c o", o=D),
            src, writes=[f"W2d{j}"] + [f"wmt{ci}" for ci in over])

    def u_tr(t):
        lm = None
        for r in range(OD):
            base = 0 if r < 16 else (32 if r < 48 else 64)
            lm = t.transpose(P[5][0:64, r:r + 1], pad_ap(r, 0, 64),
                             ones_c[base:base + 1, 0:1])
        return lm
    E("tensor", u_tr, reads=["ppad", "ppadB", "ppadC", "ones_c"],
      writes=["P5"])
    E("vector", lambda v: v.tensor_copy(Ub[:], P[5][0:64, 0:64]),
      reads=["P5"], writes=["Ub"])
    E("tensor", lambda t: t.transpose(P[5][0:64, 0:64], Ub[:], eyef[0:64, 0:64]),
      reads=["Ub", "eyef"], writes=["P5"])
    E("vector", lambda v: v.tensor_tensor(Om[:], P[5][0:64, 0:64], Ub[:], op=ALU.subtract),
      reads=["Ub", "P5"], writes=["Om"])
    E("vector", lambda v: v.tensor_tensor(OmN[:], Ub[:], P[5][0:64, 0:64], op=ALU.subtract),
      reads=["Ub", "P5"], writes=["OmN"])
    E("tensor", lambda t: t.matmul(P[5][0:64, 0:64], OmN[:], Om[:], start=True,
                                   stop=True, skip_group_check=True),
      reads=["OmN", "Om"], writes=["P5"])
    E("vector", lambda v: v.tensor_copy(Om2[:], P[5][0:64, 0:64]),
      reads=["P5"], writes=["Om2"])
    E("tensor", lambda t: t.matmul(P[5][0:64, 0:64], Om2[:], Om[:], start=True,
                                   stop=True, skip_group_check=True),
      reads=["Om2", "Om"], writes=["P5"])
    E("vector", lambda v: v.tensor_tensor(Gt[:], eyef[0:64, 0:64], Om[:], op=ALU.subtract),
      reads=["eyef", "Om"], writes=["Gt"])
    E("vector", lambda v: v.scalar_tensor_tensor(Gt[:], Om2[:], 0.5, Gt[:],
                                                 op0=ALU.mult, op1=ALU.add),
      reads=["Om2", "Gt"], writes=["Gt"])
    E("vector", lambda v: v.scalar_tensor_tensor(Gt[:], P[5][0:64, 0:64], -1.0 / 6.0,
                                                 Gt[:], op0=ALU.mult, op1=ALU.add),
      reads=["P5", "Gt"], writes=["Gt"])
    E("vector", lambda v: v.tensor_copy(Gtb[:], Gt[:]), reads=["Gt"], writes=["Gtb"])
    E("tensor", lambda t: t.matmul(P[4][0:64, 0:512], Gtb[:], ovt[:, 0:512],
                                   start=True, stop=True, skip_group_check=True),
      reads=["Gtb", "ovt"], writes=["P4"])
    E("tensor", lambda t: t.matmul(P[5][0:64, 0:512], Gtb[:], ovt[:, 512:1024],
                                   start=True, stop=True, skip_group_check=True),
      reads=["Gtb", "ovt"], writes=["P5"])
    E("tensor", lambda t: t.matmul(P[6][0:64, 0:1], ones_r[0:1, 0:64], mixs[0:1, 0:1],
                                   start=True, stop=True, skip_group_check=True),
      reads=["ones_r", "mixs"], writes=["P6"])
    E("vector", lambda v: v.tensor_copy(omx[:], P[6][0:64, 0:1]),
      reads=["P6"], writes=["omx"])
    E("vector", lambda v: v.tensor_scalar(Mp[0:64, 0:512], P[4][0:64, 0:512],
                                          omx[:, 0:1], None, op0=ALU.mult),
      reads=["P4", "omx"], writes=["Mp"])
    E("vector", lambda v: v.tensor_scalar(Mp[0:64, 512:1024], P[5][0:64, 0:512],
                                          omx[:, 0:1], None, op0=ALU.mult),
      reads=["P5", "omx"], writes=["Mp"])

    def vq_mm(t):
        lm = None
        for dc in range(DC):
            lm = t.matmul(P[6][0:64, 0:1], vint_s[:, dc * OD:(dc + 1) * OD],
                          hm_col[:, dc:dc + 1], start=(dc == 0), stop=(dc == DC - 1),
                          skip_group_check=True)
        return lm
    E("tensor", vq_mm, reads=["vint_s", "hm_col"], writes=["P6"])
    E("vector", lambda v: v.tensor_tensor(vqi[:], vqi[:], P[6][0:64, 0:1], op=ALU.add),
      reads=["P6", "vqi"], writes=["vqi"])
    E("vector", lambda v: v.tensor_tensor(cb2[:], cbt_s[:], cbt_s[:], op=ALU.mult),
      reads=["cbt_s"], writes=["cb2"])
    E("tensor", lambda t: t.matmul(P[7][0:1, 0:256], ones64[:], cb2[:], start=True,
                                   stop=True, skip_group_check=True),
      reads=["ones64", "cb2"], writes=["P7"])
    def emit_newton(qrg, q_ap, out_ap, outrg, int_ap, s1_ap, s2_ap, scr_rg):
        E("vector", lambda v: v.tensor_scalar(int_ap, q_ap.bitcast(I32), 1, None,
                                              op0=ALU.arith_shift_right),
          reads=[qrg], writes=[scr_rg])
        E("vector", lambda v: v.tensor_scalar(int_ap, int_ap, -1, MAGIC,
                                              op0=ALU.mult, op1=ALU.add),
          reads=[scr_rg], writes=[scr_rg])
        r = int_ap.bitcast(F32)
        for it in range(2):
            E("vector", lambda v: v.tensor_tensor(s1_ap, r, r, op=ALU.mult),
              reads=[scr_rg], writes=[scr_rg])
            E("vector", lambda v: v.tensor_tensor(s1_ap, s1_ap, q_ap, op=ALU.mult),
              reads=[scr_rg, qrg], writes=[scr_rg])
            E("vector", lambda v: v.tensor_scalar(s2_ap, s1_ap, -0.5, 1.5,
                                                  op0=ALU.mult, op1=ALU.add),
              reads=[scr_rg], writes=[scr_rg])
            dst, drg = (out_ap, outrg) if it == 1 else (r, scr_rg)
            E("vector", lambda v, dst=dst: v.tensor_tensor(dst, r, s2_ap, op=ALU.mult),
              reads=[scr_rg], writes=[drg])
    E("vector", lambda v: v.tensor_copy(rsn, P[7][0:1, 0:256]),
      reads=["P7"], writes=["rsn"])
    emit_newton("rsn", rsn, rsn, "rsn", ni, sc_n[0:1, 0:256],
                mask_r[0:1, 0:256], "nscr")
    E("tensor", lambda t: t.matmul(P[7][0:1, 0:256], vqi[:], cbt_s[:], start=True,
                                   stop=True, skip_group_check=True),
      reads=["vqi", "cbt_s"], writes=["P7"])
    E("vector", lambda v: v.tensor_tensor(sc_n[:], P[7][0:1, 0:256], rsn, op=ALU.mult),
      reads=["P7", "rsn"], writes=["sc_n"])
    E("vector", lambda v: v.reduce_max(sc_mx[:], sc_n[:], axis=AX.X),
      reads=["sc_n"], writes=["sc_mx"])
    E("vector", lambda v: v.tensor_scalar(mask_r[:], sc_n[:], sc_mx[0:1, 0:1], None,
                                          op0=ALU.is_ge),
      reads=["sc_n", "sc_mx"], writes=["mask_r"])
    def mask_tr(t):
        t.transpose(P[6][:, 0:1], mask_r[0:1, 0:128], eyef[0:1, 0:1])
        return t.transpose(P[6][:, 1:2], mask_r[0:1, 128:256], eyef[0:1, 0:1])
    E("tensor", mask_tr, reads=["mask_r", "eyef"], writes=["P6"])
    E("vector", lambda v: v.tensor_copy(mask_cf[:], P[6][:, 0:2]),
      reads=["P6"], writes=["mask_cf"])
    def code_mm(t):
        t.matmul(P[6][0:64, 0:1], cb_s[:, 0:64], mask_cf[:, 0:1], start=True,
                 stop=False, skip_group_check=True)
        return t.matmul(P[6][0:64, 0:1], cb_s[:, 64:128], mask_cf[:, 1:2], start=False,
                        stop=True, skip_group_check=True)
    E("tensor", code_mm, reads=["cb_s", "mask_cf"], writes=["P6"])
    E("vector", lambda v: v.tensor_copy(code_c[:], P[6][0:64, 0:1]),
      reads=["P6"], writes=["code_c"])
    def vqc_mm(t):
        t.matmul(P[6][0:1, 0:512], code_c[:], vout_s[:, 0:512], start=True,
                 stop=True, skip_group_check=True)
        return t.matmul(P[7][0:1, 0:512], code_c[:], vout_s[:, 512:1024], start=True,
                        stop=True, skip_group_check=True)
    E("tensor", vqc_mm, reads=["code_c", "vout_s"], writes=["P6", "P7"])
    bias_row = brow[0:1, D:2 * D]
    E("vector", lambda v: v.tensor_scalar(bias_row, bias_row, mixr[0:1, 0:1], None,
                                          op0=ALU.mult),
      reads=["bias_row", "mixr"], writes=["bias_row"])
    E("vector", lambda v: v.scalar_tensor_tensor(bias_row, brow[0:1, 0:D],
                                                 mixr[0:1, 1:2], bias_row,
                                                 op0=ALU.mult, op1=ALU.add),
      reads=["vob_r", "mixr", "bias_row"], writes=["bias_row"])
    for nh, pk in ((0, 6), (1, 7)):
        E("vector", lambda v, nh=nh, pk=pk: v.scalar_tensor_tensor(
            brow[0:1, D + nh * 512:D + (nh + 1) * 512], P[pk][0:1, 0:512],
            mixr[0:1, 1:2], brow[0:1, D + nh * 512:D + (nh + 1) * 512],
            op0=ALU.mult, op1=ALU.add),
          reads=[f"P{pk}", "mixr", "bias_row"], writes=["bias_row"])
    dma("sync", "cload2", brow[0:1, 0:D], b2_h, writes=["vob_r", "b2row"])
    E("vector", lambda v: v.tensor_copy(Mp[64:65, :], bias_row),
      reads=["bias_row"], writes=["Mp"])

    src0 = x_h.rearrange("(t p s) d -> t p (s d)", p=128, s=2)[0]
    dma("sync", "xinA", xb[0][:], src0, writes=["xb0"])


    def attn_steps(ti):
        sl = ti % 3
        xcur = xbl[sl]
        xreg = f"xb{sl}"
        xbf = x1bf[ti % 2]
        xbreg = f"x1bf{ti % 2}"
        steps = []
        def s_xh():
            for c in range(2):
                E("vector", lambda v, c=c, xcur=xcur: v.tensor_copy(
                    xh_bf[c][:], xcur[:, c * D:c * D + OD]),
                  reads=[xreg], writes=[f"xh_bf{c}"])
            for c in range(2):
                E("tensor", lambda t, c=c: t.transpose(
                    PB[6][0:64, 256 + c * 128:256 + (c + 1) * 128], xh_bf[c][:], eyeb[:]),
                  reads=[f"xh_bf{c}", "eyeb"], writes=["P6"])
            for c in range(2):
                E("scalar", lambda s, c=c: s.copy(
                    xhT[0:64, c * 128:(c + 1) * 128],
                    PB[6][0:64, 256 + c * 128:256 + (c + 1) * 128]),
                  reads=["P6"], writes=["xhT"])

        steps.append(s_xh)
        for c in range(2):
            for nh in range(2):
                def s_lie(c=c, nh=nh):
                    pk = 6 + nh
                    E("tensor", lambda t, c=c, nh=nh, pk=pk: t.matmul(
                        P[pk][:, 0:512], xhT[:, c * 128:(c + 1) * 128],
                        Mp[:, nh * 512:(nh + 1) * 512], start=True, stop=True,
                        skip_group_check=True),
                      reads=["xhT", "Mp"], writes=[f"P{pk}"])
                    E("vector", lambda v, c=c, nh=nh, pk=pk, xcur=xcur: v.tensor_tensor(
                        tb[:, c * D + nh * 512:c * D + (nh + 1) * 512], P[pk][:, 0:512],
                        xcur[:, c * D + nh * 512:c * D + (nh + 1) * 512], op=ALU.add),
                      reads=[f"P{pk}", xreg],
                      writes=["tb"] + (["tbA", "womb_sb"] if ti == 0 else []))
                steps.append(s_lie)
        def s_rms():
            for c in range(2):
                E("vector", lambda v, c=c: v.bn_stats(bst[:, c * 12:c * 12 + 6],
                                                      tb[:, c * D:c * D + 512]),
                  reads=["tb"], writes=["bst"])
                E("vector", lambda v, c=c: v.bn_stats(bst[:, c * 12 + 6:c * 12 + 12],
                                                      tb[:, c * D + 512:(c + 1) * D]),
                  reads=["tb"], writes=["bst"])
                E("vector", lambda v, c=c: v.bn_aggr(agg[:, c * 2:c * 2 + 2],
                                                     bst[:, c * 12:c * 12 + 12]),
                  reads=["bst"], writes=["agg"])
                E("vector", lambda v, c=c: v.tensor_tensor(
                    q_r[:, c:c + 1], agg[:, c * 2:c * 2 + 1], agg[:, c * 2:c * 2 + 1],
                    op=ALU.mult),
                  reads=["agg"], writes=["q_r"])
                E("vector", lambda v, c=c: v.tensor_tensor(
                    q_r[:, c:c + 1], q_r[:, c:c + 1], agg[:, c * 2 + 1:c * 2 + 2],
                    op=ALU.add),
                  reads=["agg", "q_r"], writes=["q_r"])
            E("vector", lambda v: v.tensor_scalar(q_r[:], q_r[:], RMS_EPS, None, op0=ALU.add),
              reads=["q_r"], writes=["q_r"])
            emit_newton("q_r", q_r[:], rstd_r[:], "rstd_r", nwi[:, 0:2],
                        nws[:, 0:2], nws[:, 2:4], "nwsc")
            for c in range(2):
                for nh in range(2):
                    s0 = c * D + nh * 512
                    E("vector", lambda v, c=c, s0=s0, nh=nh: v.scalar_tensor_tensor(
                        tb[:, s0:s0 + 512], tb[:, s0:s0 + 512], rstd_r[:, c:c + 1],
                        nw_bc[:, nh * 512:(nh + 1) * 512], op0=ALU.mult, op1=ALU.mult),
                      reads=["rstd_r", "nw_bc", "tb"], writes=["tb"])
            E("vector", lambda v, xcur=xcur: v.tensor_tensor(tb[:], tb[:], xcur,
                                                             op=ALU.add),
              reads=[xreg, "tb"], writes=["tb"])

        steps.append(s_rms)
        def s_ln1():
            for c in range(2):
                E("vector", lambda v, c=c: v.bn_stats(bst[:, c * 12:c * 12 + 6],
                                                      tb[:, c * D:c * D + 512]),
                  reads=["tb"], writes=["bst"])
                E("vector", lambda v, c=c: v.bn_stats(bst[:, c * 12 + 6:c * 12 + 12],
                                                      tb[:, c * D + 512:(c + 1) * D]),
                  reads=["tb"], writes=["bst"])
                E("vector", lambda v, c=c: v.bn_aggr(agg[:, c * 2:c * 2 + 2],
                                                     bst[:, c * 12:c * 12 + 12]),
                  reads=["bst"], writes=["agg"])
            agg_m = agg[:, 0:4].rearrange("p (c s) -> p c s", s=2)[:, :, 0]
            agg_v = agg[:, 0:4].rearrange("p (c s) -> p c s", s=2)[:, :, 1]
            E("vector", lambda v: v.tensor_scalar(q1[:], agg_v, LN_EPS, None, op0=ALU.add),
              reads=["agg"], writes=["q1"])
            E("vector", lambda v: v.tensor_scalar(negm1[:], agg_m, -1.0, None, op0=ALU.mult),
              reads=["agg"], writes=["negm1"])
            emit_newton("q1", q1[:], inv1[:], "inv1", nwi[:, 2:4],
                        nws[:, 0:2], nws[:, 2:4], "nwsc")

        steps.append(s_ln1)
        def s_ln1b():
            xbw = [xbreg] + (["scrW2"] if ti == 0 else []) \
                + (["Ub", "Om", "OmN", "Om2", "Gt", "Gtb"] if ti == 1 else [])
            for c in range(2):
                E("vector", lambda v, c=c, xbf=xbf, xbw=xbw: v.tensor_scalar(
                    xbf[:, c * D:(c + 1) * D], tb[:, c * D:(c + 1) * D],
                    negm1[:, c:c + 1], inv1[:, c:c + 1], op0=ALU.add, op1=ALU.mult),
                  reads=["tb", "negm1", "inv1"], writes=xbw)

        steps.append(s_ln1b)
        return steps


    def emit_x1T(ti):
        sl = ti % 2
        xbf = x1bf[sl]
        xbreg = f"x1bf{sl}"
        xt = x1Tb[ti % 2]
        def x1_tr(t, xbf=xbf):
            lm = None
            for bi in range(16):
                dc, c = bi // 2, bi % 2
                pb = PB[6 + bi // 8]
                lm = t.transpose(pb[:, (bi % 8) * 128:(bi % 8 + 1) * 128],
                                 xbf[:, c * D + dc * 128:c * D + (dc + 1) * 128],
                                 eyeb[:])
            return lm
        E("tensor", x1_tr, reads=[xbreg, "eyeb"], writes=["P6", "P7"])
        xw = [f"x1T{ti % 2}"] + (["scrW1"] if ti == 1 else [])
        E("scalar", lambda s, xt=xt: s.copy(xt[:, 0:1024], PB[6][:, 0:1024]),
          reads=["P6"], writes=xw)
        E("scalar", lambda s, xt=xt: s.copy(xt[:, 1024:2048], PB[7][:, 0:1024]),
          reads=["P7"], writes=xw)

    x1cur = [0]
    def emit_mm1(k):
        hp = P[4 + k % 2]
        hreg = f"P{4 + k % 2}"
        xt = x1Tb[x1cur[0]]
        def mm1(t, k=k, hp=hp, xt=xt):
            lm = None
            for dc in range(DC):
                lm = t.matmul(hp[:, 0:256],
                              W1e[:, dc * H + k * 128:dc * H + (k + 1) * 128],
                              xt[:, dc * 256:(dc + 1) * 256],
                              start=(dc == 0), stop=(dc == DC - 1),
                              skip_group_check=True)
            return lm
        E("tensor", mm1,
          reads=[f"W1k_{k // 8}_{(k % 8) // 4}", f"x1T{x1cur[0]}"], writes=[hreg])
        hb = hbf[k % 2]
        E("scalar", lambda s, k=k, hp=hp, hb=hb: s.activation(
            hb[:], hp[:, 0:256], AF.Gelu, bias=b1col[:, k:k + 1], scale=1.0),
          reads=[hreg, "b1col"], writes=[f"hbf{k % 2}"])

    def emit_mm2(k):
        hb = hbf[k % 2]
        for c in range(2):
            def mm2(t, k=k, hb=hb, c=c):
                lm = None
                for nh in range(2):
                    lm = t.matmul(FP[c][:, nh * 512:(nh + 1) * 512],
                                  hb[:, c * 128:(c + 1) * 128],
                                  W2e[:, k * D + nh * 512:k * D + (nh + 1) * 512],
                                  start=(k == 0), stop=(k == HC - 1),
                                  skip_group_check=True)
                return lm
            E("tensor", mm2, reads=[f"hbf{k % 2}", f"W2k_{k}"],
              writes=[f"F{c}a", f"F{c}b"])

    pending_out = []
    xbl = [xb[0][:], xb[1][:], scrP[:, 0:2048]]
    for _s in attn_steps(0):
        _s()
    emit_x1T(0)

    E("vector", lambda v: v.tensor_copy(scrP[0:1, 0:1], scrP[0:1, 0:1]),
      reads=["scrW0", "scrW1", "ppad", "ppadB", "ppadC", "hmr", "rsn"],
      writes=["scrB"])
    for gi, (eng, sem, lvl, l0, l1) in enumerate(
            [("sync", "rows", 0, 1, 7), ("scalar", "rowsb", 1, 7, 13),
             ("sync", "rowsg", 2, 13, 17)]):
        npair = (l1 - l0) // 2
        view = scrPb[32 * lvl:32 * lvl + 1, 0:npair * 2048].rearrange(
            "q (p c) -> q p c", c=2048)
        lo = l0
        dma(eng, sem, view[:, :, 0:1024],
            x1bf[0][lo // 2:lo // 2 + npair, D:2 * D],
            reads=["x1bf0", "scrB"],
            writes=[f"row{l}" for l in range(l0, l1, 2)])
        le = l0 + 1
        dma(eng, sem, view[:, :, 1024:2048],
            x1bf[0][le // 2:le // 2 + npair, 0:D],
            reads=["x1bf0", "scrB"],
            writes=[f"row{l}" for l in range(l0 + 1, l1, 2)])
    for l in range(2, 17, 2):
        E("vector", lambda v, l=l: v.tensor_scalar(rowap(l), rowap(l),
                                                   mixr[0:1, 2:3], None, op0=ALU.mult),
          reads=[f"row{l}", "mixr"], writes=[f"row{l}"])


    FT = [(F01, "F0a", "F0b"), (F23, "F1a", "F1b")]
    _wu = [0]

    def w1_pair(i, dc):
        n = _wu[0]
        _wu[0] += 1
        col = dc * H + i * D
        fp, ra, rb = FT[n % 2]
        def mms(t, i=i, dc=dc, fp=fp):
            lm = None
            for half in range(2):
                lm = t.matmul(fp[:, half * 512:(half + 1) * 512],
                              rowap(2 + 2 * i)[0:1, dc * 128:(dc + 1) * 128],
                              rowap(1 + 2 * i)[0:1, half * 512:(half + 1) * 512],
                              start=True, stop=True, skip_group_check=True)
            return lm
        E("tensor", mms,
          reads=[f"row{2 + 2 * i}", f"row{1 + 2 * i}"],
          writes=[ra, rb])
        E("vector", lambda v, fp=fp, col=col: v.tensor_tensor(
            W1e[:, col:col + 1024], fp[:, 0:1024],
            W1e[:, col:col + 1024], op=ALU.add),
          reads=[ra, rb, f"W1d{dc}"], writes=[f"W1k_{i}_0", f"W1k_{i}_1"])

    def w2_quad(hc):
        n = _wu[0]
        _wu[0] += 1
        j = hc // 8
        col = hc * D
        fp, ra, rb = FT[n % 2]
        def mms(t, hc=hc, j=j, fp=fp):
            lm = None
            for i in range(4):
                evs = rowap(9 + 2 * i).rearrange("p (q j) -> p q j", j=4)[:, :, j]
                lm = t.matmul(fp[:, i * 256:(i + 1) * 256],
                              rowap(10 + 2 * i)[0:1,
                                                (hc % 8) * 128:(hc % 8 + 1) * 128],
                              evs, start=(i % 2 == 0), stop=(i % 2 == 1),
                              skip_group_check=True)
            return lm
        E("tensor", mms,
          reads=([f"row{9 + 2 * i}" for i in range(4)]
                 + [f"row{10 + 2 * i}" for i in range(4)]),
          writes=[ra, rb])
        E("vector", lambda v, fp=fp, col=col: v.tensor_tensor(
            W2e[:, col:col + 1024], fp[:, 0:1024],
            W2e[:, col:col + 1024], op=ALU.add),
          reads=[ra, rb, f"W2d{hc // 4}"], writes=[f"W2k_{hc}"])

    for g in range(8):
        gi, ghalf = g // 2, g % 2
        if ghalf == 0:
            for dc in range(DC):
                w1_pair(gi, dc)
        for hc in range(4 * g, 4 * g + 4):
            w2_quad(hc)

    for nh in range(2):
        E("tensor", lambda t, nh=nh: t.matmul(
            P[6 + nh][:, 0:512], ones_r[:], brow[0:1, nh * 512:(nh + 1) * 512],
            start=True, stop=True, skip_group_check=True),
          reads=["ones_r", "b2row"] + [f"row{l}" for l in range(1, 17)],
          writes=[f"P{6 + nh}"])
        E("vector", lambda v, nh=nh: v.tensor_copy(
            b2_bc[:, nh * 512:(nh + 1) * 512], P[6 + nh][:, 0:512]),
          reads=[f"P{6 + nh}"], writes=["b2bc"])
    for c in range(2):
        E("gpsimd", lambda g, c=c: g.tensor_tensor(
            x1bf[0][:, c * D:(c + 1) * D], x1bf[0][:, c * D:(c + 1) * D],
            b2_bc[:], op=ALU.add),
          reads=["b2bc", "x1bf0"], writes=["x1bf0"])


    for ti in range(NT):
        x1cur[0] = ti % 2
        sl = ti % 3
        xcur = xbl[sl]
        xreg = f"xb{sl}"
        xbf = x1bf[ti % 2]
        xbreg = f"x1bf{ti % 2}"
        if ti + 1 < NT:
            nsl = (ti + 1) % 3
            nxt = x_h.rearrange("(t p s) d -> t p (s d)", p=128, s=2)[ti + 1]
            extra = (["scrB", "ppad", "ppadB", "ppadC", "rsn", "womb_sb"]
                     + [f"row{l}" for l in range(1, 17)]) if ti + 1 == 2 else []
            dma("sync", "xinA" if nsl == 0 else ("xinB" if nsl == 1 else "xinC"),
                xbl[nsl], nxt,
                writes=[f"xb{nsl}"] + extra)
            for dst, srcap, xr in pending_out:
                dma("sync", "dout", dst, srcap, reads=[xr])
            pending_out.clear()
        emit_mm1(0)
        emit_mm1(1)
        steps = attn_steps(ti + 1) if ti + 1 < NT else []
        for k in range(HC):
            emit_mm2(k)
            if k + 2 < HC:
                emit_mm1(k + 2)
            if steps and 8 <= k < 8 + len(steps) - 1:
                steps[k - 8]()
            if steps and k == 24:
                steps[-1]()
            if k == 26 and ti + 1 < NT:
                emit_x1T(ti + 1)
                nbf = x1bf[(ti + 1) % 2]
                nbreg = f"x1bf{(ti + 1) % 2}"
                for c in range(2):
                    E("gpsimd", lambda g, nbf=nbf, c=c: g.tensor_tensor(
                        nbf[:, c * D:(c + 1) * D], nbf[:, c * D:(c + 1) * D],
                        b2_bc[:], op=ALU.add),
                      reads=["b2bc", nbreg], writes=[nbreg])
        for c in range(2):
            E("vector", lambda v, c=c, xcur=xcur, xbf=xbf: v.tensor_tensor(
                xcur[:, c * D:(c + 1) * D], FP[c][:, 0:1024],
                xbf[:, c * D:(c + 1) * D], op=ALU.add),
              reads=[f"F{c}a", f"F{c}b", xbreg], writes=[xreg])
        for c in range(2):
            E("vector", lambda v, c=c, xcur=xcur: v.bn_stats(
                bst2[:, c * 12:c * 12 + 6], xcur[:, c * D:c * D + 512]),
              reads=[xreg], writes=["bst2"])
            E("vector", lambda v, c=c, xcur=xcur: v.bn_stats(
                bst2[:, c * 12 + 6:c * 12 + 12], xcur[:, c * D + 512:(c + 1) * D]),
              reads=[xreg], writes=["bst2"])
            E("vector", lambda v, c=c: v.bn_aggr(agg2[:, c * 2:c * 2 + 2],
                                                 bst2[:, c * 12:c * 12 + 12]),
              reads=["bst2"], writes=["agg2"])
        agg2_m = agg2[:, 0:4].rearrange("p (c s) -> p c s", s=2)[:, :, 0]
        agg2_v = agg2[:, 0:4].rearrange("p (c s) -> p c s", s=2)[:, :, 1]
        E("vector", lambda v: v.tensor_scalar(q2[:], agg2_v, LN_EPS, None, op0=ALU.add),
          reads=["agg2"], writes=["q2"])
        E("vector", lambda v: v.tensor_scalar(negm2[:], agg2_m, -1.0, None, op0=ALU.mult),
          reads=["agg2"], writes=["negm2"])
        emit_newton("q2", q2[:], inv2[:], "inv2", nwi[:, 4:6],
                    nws[:, 0:2], nws[:, 2:4], "nwsc")
        for c in range(2):
            E("vector", lambda v, c=c, xcur=xcur: v.tensor_scalar(
                xcur[:, c * D:(c + 1) * D], xcur[:, c * D:(c + 1) * D],
                negm2[:, c:c + 1], inv2[:, c:c + 1], op0=ALU.add, op1=ALU.mult),
              reads=["negm2", "inv2", xreg], writes=[xreg])
        dst = out_h[ti * TILE:(ti + 1) * TILE, :].rearrange("(p s) d -> p (s d)", s=2)
        pending_out.append((dst, xcur, xreg))


    for dst, srcap, xreg in pending_out:
        dma("sync", "dout", dst, srcap, reads=[xreg])
    pending_out.clear()
    S.wait_only("sync", [("dout", S.cnt.get("dout", 0))])

    sems = {}
    for name in S.cnt:
        sems[name] = es.enter_context(nc.semaphore(name))
    block = es.enter_context(nc.Block())
    with es:
        S.replay(block, sems)
    return nc


_CACHE = {}

_TRIU_R, _TRIU_C = np.triu_indices(OD, k=1)

def _pad_womb(womb):
    out = np.zeros((1, OD * OD), np.float32)
    out[0, _TRIU_R * OD + _TRIU_C] = womb.ravel()
    return out




def kernel(**inputs):
    if "nc" not in _CACHE:
        _CACHE["nc"] = build_nc()
    nc = _CACHE["nc"]

    f32 = np.float32
    shared = dict(
        w1t=np.ascontiguousarray(np.asarray(inputs["L0_ffn_W1"], f32).T),
        w2t=np.ascontiguousarray(np.asarray(inputs["L0_ffn_W2"], f32).T),
        womt=np.ascontiguousarray(np.asarray(inputs["L0_attn_omega_proj_w"], f32).T),
        womb=np.asarray(inputs["L0_attn_omega_proj_b"], f32).reshape(1, NPARAMS),
        ooutt=np.ascontiguousarray(np.asarray(inputs["L0_attn_omega_out_w"], f32).T),
        oob=np.asarray(inputs["L0_attn_omega_out_b"], f32).reshape(1, D),
        vint=np.ascontiguousarray(np.asarray(inputs["L0_attn_vq_proj_in_w"], f32).T),
        vib=np.asarray(inputs["L0_attn_vq_proj_in_b"], f32).reshape(OD, 1),
        voutt=np.ascontiguousarray(np.asarray(inputs["L0_attn_vq_proj_out_w"], f32).T),
        vob=np.asarray(inputs["L0_attn_vq_proj_out_b"], f32).reshape(1, D),
        cb=np.ascontiguousarray(np.asarray(inputs["L0_attn_codebook"], f32)),
        cbt=np.ascontiguousarray(np.asarray(inputs["L0_attn_codebook"], f32).T),
        nw=np.asarray(inputs["L0_attn_norm_w"], f32).reshape(1, D),
        b1=np.asarray(inputs["L0_ffn_b1"], f32).reshape(1, H),
        b2=np.asarray(inputs["L0_ffn_b2"], f32).reshape(1, D),
        mixes=np.array([[float(np.asarray(inputs["L0_attn_omega_mix"]))],
                        [float(np.asarray(inputs["L0_attn_vq_mix"]))],
                        [float(np.asarray(inputs["L0_ffn_scale"]))]], dtype=f32),
        eye=np.eye(128, dtype=f32),
    )
    X = np.asarray(inputs["L0_attn_x"], f32)
    in_maps = []
    for b in range(B):
        m = dict(shared)
        m["x"] = np.ascontiguousarray(X[b])
        in_maps.append(m)

    res = run_bass_kernel_spmd(nc, in_maps, core_ids=list(range(B)))
    out = np.stack([np.asarray(res.results[b]["out"]) for b in range(B)], axis=0)
    return out.astype(np.float32)
